# version 05
# baseline (speedup 1.0000x reference)
import math
from contextlib import ExitStack

import numpy as np

import concourse.bass as bass
import concourse.mybir as mybir
from concourse.bass_utils import run_bass_kernel_spmd

F32 = mybir.dt.float32
BF16 = mybir.dt.bfloat16
AF = mybir.ActivationFunctionType
ALU = mybir.AluOpType

D = 2048
KC = 16
TLOC = 2048
TS = 512
NSL = 4
NH = 16
HD = 128
NE = 8
FCH = 11
FSCH = 22
NEG = -1.0e6

N_GU = NE * FCH * 2 * KC
N_SGU = FSCH * 2 * KC
N_DOWN = KC * (NE * FCH + FSCH)
SEG_QK0 = N_GU + N_SGU + N_DOWN
SEG_V0 = SEG_QK0 + 32 * KC
SEG_WO0 = SEG_V0 + KC * 16
NTILES = SEG_WO0 + 16 * KC

RING = 80

_PHASES = ("ffn1", "attn", "ffn3")
_DETECT_RACES = False
_DEBUG_TAPS = False


def _gu_tile(n, f, which, h):
    return (n * FCH + f) * 32 + which * 16 + h


def _sgu_tile(fs, which, h):
    return N_GU + fs * 32 + which * 16 + h


def _down_tile(h, j):
    return N_GU + N_SGU + h * (NE * FCH + FSCH) + j


def _qk_tile(oc, h):
    return SEG_QK0 + oc * 16 + h


def _v_tile(d, vc):
    return SEG_V0 + d * 16 + vc


def _wo_tile(oc, k):
    return SEG_WO0 + oc * 16 + k


class Sem:
    __slots__ = ("h", "n", "name")

    def __init__(self, mk, name):
        self.h = mk.ctx.enter_context(mk.nc.semaphore(name))
        self.n = 0
        self.name = name

    def inc(self, amt=1):
        self.n += amt
        return self.n


class Bank:

    def __init__(self, ap, parts):
        self.ap = ap
        self.parts = parts

    def acquire(self):
        w = []
        for p in self.parts:
            w.extend(p.free_at)
            p.free_at = []
        return w

    def release(self, sem, val):
        for p in self.parts:
            p.free_at = [(sem, val)]


def _merge(*wait_lists):
    best = {}
    for wl in wait_lists:
        for (sem, val) in wl:
            if val > best.get(sem, -1):
                best[sem] = val
    return [(s, v) for s, v in best.items()]


class MK:
    def __init__(self, nc, ctx):
        self.nc = nc
        self.ctx = ctx
        self.plan = []
        for nm in ("s_pe", "s_act", "s_dve", "s_cck", "s_ccv"):
            setattr(self, nm, Sem(self, nm))
        self.pools = {}
        for (nm, queue, n) in [("w", "sync", 8), ("wg", "gpsimd", 4), ("tee", "sync", 4),
                               ("x", "sync", 6), ("kv", "sync", 4), ("khl", "sync", 8),
                               ("out", "sync", 4)]:
            self.pools[nm] = (queue, [Sem(self, f"p_{nm}{i}") for i in range(n)], [0])
        self.ring_pos = 0
        self.ring_free = [[] for _ in range(RING)]
        self.ring_extra = {}
        self.first_use = np.zeros(NTILES, dtype=bool)
        self.tee_val = [None] * NTILES
        self.pend = []
        self.g = {}
        self.pool = ctx.enter_context(nc.psum_tensor("pool", [128, 4096], F32))[:]
        self.banks = [Bank(self.pool[:, 512 * i:512 * (i + 1)], None) for i in range(8)]
        for b in self.banks:
            b.parts = [b]
            b.free_at = []

    def region(self, lo, hi):
        return Bank(self.pool[:, 512 * lo:512 * hi], [self.banks[i] for i in range(lo, hi)])

    def step(self, eng, fn, waits=(), incs=()):
        for (sem, val) in waits:
            assert val <= sem.n, f"wait {sem.name}={val} > emitted {sem.n}"
        self.plan.append((eng, fn, tuple(waits), tuple(incs)))

    def dma(self, pool, fn, waits=()):
        queue, sems, idx = self.pools[pool]
        s = sems[idx[0] % len(sems)]
        idx[0] += 1
        self.step(queue, fn, waits=_merge(list(waits), [(s, s.n)]), incs=[(s, 16)])
        return (s, s.inc(16))

    def pool_now(self, pool):
        return [(s, s.n) for s in self.pools[pool][1] if s.n > 0]

    def t_inc(self):
        eng, fn, waits, incs = self.plan[-1]
        assert eng == "tensor", eng
        self.plan[-1] = (eng, fn, waits, incs + ((self.s_pe, 1),))
        return (self.s_pe, self.s_pe.inc())

    def fetch(self, tile_ids, align16=False):
        assert len(tile_ids) <= RING - 16
        if align16 and (self.ring_pos % 16) != 0:
            self.flush()
            self.ring_pos += 16 - self.ring_pos % 16
        slots = []
        for t in tile_ids:
            self.pend.append((t, self.ring_pos % RING))
            slots.append(self.ring_pos % RING)
            self.ring_pos += 1
        wv = self.flush()
        return slots, wv

    def flush(self):
        if not self.pend:
            return []
        pend, self.pend = self.pend, []
        i = 0
        out = []
        while i < len(pend):
            j = i + 1
            t0 = pend[i][0]
            while (j < len(pend)
                   and pend[j][0] == pend[j - 1][0] + 1
                   and pend[j][1] == pend[j - 1][1] + 1
                   and bool(self.first_use[pend[j][0]]) == bool(self.first_use[t0])):
                j += 1
            out.append(self._emit_run(pend[i:j]))
            i = j
        return _merge(*[[w] for w in out])

    def _emit_run(self, run):
        t0, sl0 = run[0]
        n = len(run)
        is_first = not self.first_use[t0]
        waits = _merge(*[self.ring_free[sl0 + k] for k in range(n)])
        dst = self.g["wring"][:, sl0 * 128:(sl0 + n) * 128]
        if is_first:
            src = self.g["wblob"][:, t0 * 128:(t0 + n) * 128]
            wv = self.dma("wg", lambda e, d=dst, s=src: e.dma_start(out=d, in_=s), waits=waits)
            tdst = self.g["wbf16"][:, t0 * 128:(t0 + n) * 128]
            tv = self.dma("tee", lambda e, d=tdst, s=dst: e.dma_start(out=d, in_=s), waits=[wv])
            for (t, sl) in run:
                self.first_use[t] = True
                self.tee_val[t] = tv
                self.ring_extra[sl] = tv
            return wv
        else:
            src = self.g["wbf16"][:, t0 * 128:(t0 + n) * 128]
            waits = _merge(waits, *[[self.tee_val[t]] for t, _ in run])
            return self.dma("w", lambda e, d=dst, s=src: e.dma_start(out=d, in_=s), waits=waits)

    def ws_mark(self, slots, sem, val):
        for sl in slots:
            ex = self.ring_extra.pop(sl, None)
            self.ring_free[sl] = [(sem, val)] + ([ex] if ex else [])

    def wtile(self, slot, n=1):
        return self.g["wring"][:, slot * 128:(slot + n) * 128]



def build_program(phases=_PHASES):
    nc = bass.Bass("TRN2", target_bir_lowering=False, detect_race_conditions=_DETECT_RACES)
    ctx = ExitStack()
    mk = MK(nc, ctx)
    g = mk.g

    g["x_in"] = nc.dram_tensor("x_in", [D, TLOC], F32, kind="ExternalInput").ap()
    g["wblob"] = nc.dram_tensor("wblob", [128, NTILES * 128], F32, kind="ExternalInput").ap()
    g["gwpm"] = nc.dram_tensor("gwpm", [128, 128], F32, kind="ExternalInput").ap()
    g["consts"] = nc.dram_tensor("consts", [128, 1664], F32, kind="ExternalInput").ap()
    g["mask0"] = nc.dram_tensor("mask0", [128, 128], F32, kind="ExternalInput").ap()
    g["mask1"] = nc.dram_tensor("mask1", [128, 128], F32, kind="ExternalInput").ap()
    g["ctab_d"] = nc.dram_tensor("ctab", [128, TLOC], F32, kind="ExternalInput").ap()
    g["stab_d"] = nc.dram_tensor("stab", [128, TLOC], F32, kind="ExternalInput").ap()
    g["out"] = nc.dram_tensor("out_x", [D, TLOC], F32, kind="ExternalOutput").ap()
    if _DEBUG_TAPS:
        g["dbg_h"] = nc.dram_tensor("dbg_h", [128, KC * TS], BF16, kind="ExternalOutput").ap()
        g["dbg_l8"] = nc.dram_tensor("dbg_l8", [8, TS], F32, kind="ExternalOutput").ap()
        g["dbg_w8"] = nc.dram_tensor("dbg_w8", [8, TS], F32, kind="ExternalOutput").ap()
        g["dbg_wbc"] = nc.dram_tensor("dbg_wbc", [128, NE * TS], BF16, kind="ExternalOutput").ap()
        g["dbg_a"] = nc.dram_tensor("dbg_a", [128, 11 * TS], BF16, kind="ExternalOutput").ap()
        g["dbg_as"] = nc.dram_tensor("dbg_as", [128, 4 * TS], BF16, kind="ExternalOutput").ap()
        g["dbg_abc"] = nc.dram_tensor("dbg_abc", [128, 2 * TS], F32, kind="ExternalOutput").ap()
        g["dbg_rows"] = nc.dram_tensor("dbg_rows", [1, 4 * TS], F32, kind="ExternalOutput").ap()

    g["wbf16"] = nc.dram_tensor("wbf16", [128, NTILES * 128], BF16).ap()
    g["xres1"] = nc.dram_tensor("xres1", [D, TLOC], F32).ap()
    g["xres2"] = nc.dram_tensor("xres2", [D, TLOC], F32).ap()
    g["k_own"] = nc.dram_tensor("k_own", [D, TLOC], BF16)
    g["v_own"] = nc.dram_tensor("v_own", [TLOC, D], BF16)
    g["k_full"] = nc.dram_tensor("k_full", [2 * D, TLOC], BF16)
    g["v_full"] = nc.dram_tensor("v_full", [2 * TLOC, D], BF16)

    e = ctx.enter_context
    g["wring"] = e(nc.sbuf_tensor("wring", [128, RING * 128], BF16))[:]
    g["gw_sb"] = e(nc.sbuf_tensor("gw_sb", [128, 128], F32))[:]
    g["E8"] = e(nc.sbuf_tensor("E8", [8, 1024], F32))[:]
    g["I8"] = e(nc.sbuf_tensor("I8", [8, 8], F32))[:]
    g["I128f"] = e(nc.sbuf_tensor("I128f", [128, 128], F32))[:]
    g["I128b"] = e(nc.sbuf_tensor("I128b", [128, 128], BF16))[:]
    g["onesb"] = e(nc.sbuf_tensor("onesb", [128, 8], BF16))[:]
    g["onesrow"] = e(nc.sbuf_tensor("onesrow", [1, 128], F32))[:]
    g["onescol"] = e(nc.sbuf_tensor("onescol", [128, 1], F32))[:]
    g["onescol8"] = e(nc.sbuf_tensor("onescol8", [128, 8], F32))[:]
    g["colsumT"] = e(nc.sbuf_tensor("colsumT", [1, 8], F32))[:]
    g["mp0"] = e(nc.sbuf_tensor("mp0", [128, 128], BF16))[:]
    g["mp1"] = e(nc.sbuf_tensor("mp1", [128, 128], BF16))[:]
    g["mneg"] = e(nc.sbuf_tensor("mneg", [128, 128], BF16))[:]

    for (dst, src, q) in [
        (g["gw_sb"], g["gwpm"], "sync"),
        (g["E8"], g["consts"][0:8, 0:1024], "sync"),
        (g["I8"], g["consts"][0:8, 1024:1032], "sync"),
        (g["I128f"], g["consts"][0:128, 1152:1280], "sync"),
        (g["I128b"], g["consts"][0:128, 1152:1280], "gpsimd"),
        (g["onesb"], g["consts"][0:128, 1432:1440], "gpsimd"),
        (g["onescol"], g["consts"][0:128, 1440:1441], "sync"),
        (g["onescol8"], g["consts"][0:128, 1424:1432], "sync"),
        (g["onesrow"], g["consts"][0:1, 1281:1409], "sync"),
        (g["mp0"], g["mask0"], "gpsimd"),
        (g["mp1"], g["mask1"], "gpsimd"),
        (g["mneg"], g["consts"][0:128, 1536:1664], "gpsimd"),
    ]:
        mk.dma("x" if q == "sync" else "wg",
               lambda e2, d=dst, s=src: e2.dma_start(out=d, in_=s, allow_slow_non_contiguous=True))
    const_w = _merge(mk.pool_now("x"), mk.pool_now("wg"))

    bar = list(const_w)
    if "ffn1" in phases:
        dst = g["xres1"] if "attn" in phases else g["out"]
        _emit_ffn(mk, g["x_in"], dst, bar, colsum=True, tag="f1")
        bar = _bar(mk)
    if "attn" in phases:
        src = g["xres1"] if "ffn1" in phases else g["x_in"]
        dst = g["xres2"] if "ffn3" in phases else g["out"]
        _emit_attn(mk, src, dst, _merge(bar, const_w))
        bar = _bar(mk)
    if "ffn3" in phases:
        src = g["xres2"] if "attn" in phases else g["x_in"]
        _emit_ffn(mk, src, g["out"], bar, colsum="ffn1" not in phases, tag="f3")

    mk.step("sync", lambda e2: e2.nop(), waits=mk.pool_now("out"))

    block = ctx.enter_context(nc.Block())
    for name in ("sync", "gpsimd", "tensor", "scalar", "vector"):
        def emitter(eng, name=name):
            for (e2, fn, waits, incs) in mk.plan:
                if e2 != name:
                    continue
                for (sem, v) in waits:
                    eng.wait_ge(sem.h, v)
                inst = fn(eng)
                for (sem, amt) in incs:
                    inst.then_inc(sem.h, amt)
        getattr(block, name)(emitter)
    ctx.close()
    return nc


def _bar(mk):
    return _merge([(mk.s_pe, mk.s_pe.n), (mk.s_act, mk.s_act.n), (mk.s_dve, mk.s_dve.n)],
                  mk.pool_now("out"))



def _ln_rows(mk, rows, prev_act):
    s_act, s_dve = mk.s_act, mk.s_dve
    w0 = [prev_act] if prev_act else []
    mk.step("vector", lambda e: e.tensor_scalar(rows["negm"], rows["s1r"], -1.0 / D, None, ALU.mult), waits=w0)
    mk.step("vector", lambda e: e.tensor_scalar(rows["rt1"], rows["s2r"], 1.0 / D, 1e-5, ALU.mult, ALU.add))
    mk.step("vector", lambda e: e.tensor_tensor(rows["rt2"], rows["negm"], rows["negm"], ALU.mult))
    mk.step("vector", lambda e: e.tensor_tensor(rows["rt1"], rows["rt1"], rows["rt2"], ALU.subtract),
            incs=[(s_dve, 1)]); s_dve.inc()
    mk.step("scalar", lambda e: e.activation(rows["rt2"], rows["rt1"], AF.Ln),
            waits=[(s_dve, s_dve.n)])
    mk.step("scalar", lambda e: e.activation(rows["rstd"], rows["rt2"], AF.Exp, scale=-0.5),
            incs=[(s_act, 1)]); s_act.inc()
    rstd_act = (s_act, s_act.n)
    mk.step("vector", lambda e: e.tensor_tensor(rows["brow"], rows["negm"], rows["rstd"], ALU.mult),
            waits=[rstd_act], incs=[(s_dve, 1)]); s_dve.inc()
    return (s_dve, s_dve.n), rstd_act


def _bcast_rows(mk, b0, b1, asrc, bsrc, abc, rowm):
    g = mk.g
    s_pe, s_dve = mk.s_pe, mk.s_dve
    w = b0.acquire()
    mk.step("tensor", lambda e, a=asrc: e.matmul(b0.ap, g["onesrow"], a, start=True, stop=True),
            waits=_merge([rowm], w), incs=[(s_pe, 1)]); s_pe.inc()
    mk.step("vector", lambda e: e.tensor_copy(abc[:, 0:TS], b0.ap), waits=[(s_pe, s_pe.n)],
            incs=[(s_dve, 1)]); s_dve.inc()
    b0.release(s_dve, s_dve.n)
    w = b1.acquire()
    mk.step("tensor", lambda e, b=bsrc: e.matmul(b1.ap, g["onesrow"], b, start=True, stop=True),
            waits=_merge([rowm], w), incs=[(s_pe, 1)]); s_pe.inc()
    mk.step("vector", lambda e: e.tensor_copy(abc[:, TS:2 * TS], b1.ap), waits=[(s_pe, s_pe.n)],
            incs=[(s_dve, 1)]); s_dve.inc()
    b1.release(s_dve, s_dve.n)
    return (s_dve, s_dve.n)



def _emit_ffn(mk, src, dst, bar, colsum, tag="f1"):
    nc, g = mk.nc, mk.g
    s_pe, s_act, s_dve = mk.s_pe, mk.s_act, mk.s_dve
    B = mk.banks
    with ExitStack() as ph:
        e = ph.enter_context
        xring = e(nc.sbuf_tensor(tag + "f_xring", [128, 4 * TS], F32))[:]
        xbr = e(nc.sbuf_tensor(tag + "f_xbr", [128, 2 * TS], BF16))[:]
        hsl = e(nc.sbuf_tensor(tag + "f_h", [128, KC * TS], BF16))[:]
        sqr = e(nc.sbuf_tensor(tag + "f_sq", [128, TS], F32))[:]
        tsilu = e(nc.sbuf_tensor(tag + "f_ts", [128, 2 * TS], BF16))[:]
        ttf = e(nc.sbuf_tensor(tag + "f_ttf", [128, 2 * TS], F32))[:]
        aall = e(nc.sbuf_tensor(tag + "f_aall", [128, 88 * TS], BF16))[:]
        ash = e(nc.sbuf_tensor(tag + "f_ash", [128, FSCH * TS], BF16))[:]
        wbc = e(nc.sbuf_tensor(tag + "f_wbc", [128, NE * TS], BF16))[:]
        abc = e(nc.sbuf_tensor(tag + "f_abc", [128, 2 * TS], F32))[:]
        l8 = e(nc.sbuf_tensor(tag + "f_l8", [8, TS], F32))[:]
        ltok = e(nc.sbuf_tensor(tag + "f_ltok", [128, 32], F32))[:]
        rstdT = e(nc.sbuf_tensor(tag + "f_rstdT", [128, 4], F32))[:]
        rt = e(nc.sbuf_tensor(tag + "f_rt", [128, 32], F32))[:]
        w8 = e(nc.sbuf_tensor(tag + "f_w8", [8, TS], F32))[:]
        xout = e(nc.sbuf_tensor(tag + "f_xout", [128, 2 * TS], F32))[:]
        xdr = e(nc.sbuf_tensor(tag + "f_xdr", [128, 2 * TS], F32))[:]
        rows = {nm: e(nc.sbuf_tensor(tag + "f_" + nm, [1, TS], F32))[:]
                for nm in ("s1r", "s2r", "rt1", "rt2", "negm", "rstd")}
        rows["brow"] = rows["rt2"]

        bM = [B[0], B[1], B[2]]
        bG = [B[3], B[4]]
        bU = [B[5], B[6]]

        if colsum:
            w = _merge(bM[0].acquire(), bar)
            gw = g["gw_sb"]
            for k in range(KC):
                mk.step("tensor", lambda e2, l=gw[:, 8 * k:8 * k + 8], st=(k == 0), sp=(k == KC - 1):
                        e2.matmul(bM[0].ap[0:8, 0:1], l, g["onescol"], start=st, stop=sp),
                        waits=w if k == 0 else ())
            cs_pe = mk.t_inc()
            mk.step("vector", lambda e2: e2.tensor_copy(l8[0:8, 0:1], bM[0].ap[0:8, 0:1]),
                    waits=[cs_pe], incs=[(s_dve, 1)]); s_dve.inc()
            bM[0].release(s_dve, s_dve.n)
            w = _merge(bM[1].acquire(), [(s_dve, s_dve.n)])
            mk.step("tensor", lambda e2: e2.matmul(bM[1].ap[0:1, 0:8], l8[0:8, 0:1], g["I8"], start=True, stop=True),
                    waits=w, incs=[(s_pe, 1)]); s_pe.inc()
            mk.step("vector", lambda e2: e2.tensor_copy(g["colsumT"], bM[1].ap[0:1, 0:8]),
                    waits=[(s_pe, s_pe.n)], incs=[(s_dve, 1)]); s_dve.inc()
            bM[1].release(s_dve, s_dve.n)

        free_xring = [list(bar) for _ in range(4)]
        free_xb = [list(bar), list(bar)]
        free_sq = [[], []]
        free_ts = [[], []]
        free_tt = [[], []]
        free_h = list(bar)
        free_rows = list(bar)
        free_aw = list(bar)
        free_xdr = [list(bar), list(bar)]
        free_xout = [[], []]
        prev_rstd_act = None
        prev_exp_act = None

        for s in range(NSL):
            lgB = bM[2]
            lgw = lgB.acquire()
            s1B, s2B = bM[0], bM[1]
            s1w, s2w = s1B.acquire(), s2B.acquire()
            for k in range(KC):
                xs = xring[:, (k % 4) * TS:(k % 4 + 1) * TS]
                xw = mk.dma("x", lambda e2, d=xs, sp=src[128 * k:128 * (k + 1), TS * s:TS * (s + 1)]:
                            e2.dma_start(out=d, in_=sp),
                            waits=_merge(free_xring[k % 4]))
                sq = sqr[:, 0:TS]
                mk.step("vector", lambda e2, d=sq, a=xs: e2.tensor_tensor(d, a, a, ALU.mult),
                        waits=_merge([xw], free_sq[0]), incs=[(s_dve, 1)]); s_dve.inc()
                sq_done = (s_dve, s_dve.n)
                mk.step("tensor", lambda e2, a=xs, st=(k == 0), sp=(k == KC - 1):
                        e2.matmul(s1B.ap[0:8, :], g["onescol8"], a, start=st, stop=sp),
                        waits=_merge([xw], s1w if k == 0 else []))
                mk.step("tensor", lambda e2, a=sq, st=(k == 0), sp=(k == KC - 1):
                        e2.matmul(s2B.ap[0:8, :], g["onescol8"], a, start=st, stop=sp),
                        waits=_merge([sq_done], s2w if k == 0 else []))
                mk.step("tensor", lambda e2, l=g["gw_sb"][:, 8 * k:8 * k + 8], a=xs, st=(k == 0):
                        e2.matmul(lgB.ap[0:8, :], l, a, start=st, stop=False),
                        waits=_merge([xw], lgw if k == 0 else []), incs=[(s_pe, 1)]); s_pe.inc()
                free_xring[k % 4] = [(s_pe, s_pe.n)]
                free_sq[0] = [(s_pe, s_pe.n)]
            stats_pe = (s_pe, s_pe.n)

            mk.step("vector", lambda e2: e2.tensor_copy(rows["s1r"], s1B.ap[0:1, :]),
                    waits=_merge([stats_pe], free_rows), incs=[(s_dve, 1)]); s_dve.inc()
            mk.step("vector", lambda e2: e2.tensor_copy(rows["s2r"], s2B.ap[0:1, :]),
                    incs=[(s_dve, 1)]); s_dve.inc()
            s1B.release(s_dve, s_dve.n)
            s2B.release(s_dve, s_dve.n)
            rowm, rstd_act = _ln_rows(mk, rows, prev_rstd_act)
            prev_rstd_act = rstd_act
            mk.step("tensor", lambda e2: e2.matmul(lgB.ap[0:8, :], g["colsumT"], rows["negm"], start=False, stop=True),
                    waits=[rowm], incs=[(s_pe, 1)]); s_pe.inc()
            mk.step("vector", lambda e2: e2.tensor_copy(l8, lgB.ap[0:8, :]),
                    waits=[(s_pe, s_pe.n)], incs=[(s_dve, 1)]); s_dve.inc()
            lgB.release(s_dve, s_dve.n)
            ab_done = _bcast_rows(mk, bM[0], bM[1], rows["rstd"], rows["brow"], abc, rowm)

            for tc in range(4):
                tb = bM[2] if tc % 2 == 0 else bM[0]
                tw = tb.acquire()
                mk.step("tensor", lambda e2, l=l8[:, 128 * tc:128 * (tc + 1)], ap=tb.ap:
                        e2.matmul(ap[:, 0:8], l, g["I8"], start=True, stop=True),
                        waits=_merge(tw, [(s_dve, s_dve.n)]))
                mk.step("tensor", lambda e2, r=rows["rstd"][0:1, 128 * tc:128 * (tc + 1)], ap=tb.ap:
                        e2.matmul(ap[:, 8:9], r, g["onesrow"][0:1, 0:1], start=True, stop=True),
                        incs=[(s_pe, 1)]); s_pe.inc()
                mk.step("vector", lambda e2, d=ltok[:, 8 * tc:8 * (tc + 1)], ap=tb.ap: e2.tensor_copy(d, ap[:, 0:8]),
                        waits=[(s_pe, s_pe.n)])
                mk.step("vector", lambda e2, d=rstdT[:, tc:tc + 1], ap=tb.ap: e2.tensor_copy(d, ap[:, 8:9]),
                        incs=[(s_dve, 1)]); s_dve.inc()
                tb.release(s_dve, s_dve.n)
            wtok_done = None
            for tc in range(4):
                lt = ltok[:, 8 * tc:8 * (tc + 1)]
                r1 = rt[:, 0:8]; r2 = rt[:, 8:16]; r3 = rt[:, 16:24]
                m1 = rt[:, 24:25]; m2 = rt[:, 25:26]
                rs = rstdT[:, tc:tc + 1]
                w0 = [prev_exp_act] if prev_exp_act else []

                def vstep(fn, extra=()):
                    mk.step("vector", fn, waits=_merge([(s_dve, s_dve.n)], extra),
                            incs=[(s_dve, 1)])
                    s_dve.inc()

                vstep(lambda e2, lt=lt, rs=rs: e2.tensor_scalar(r1, lt, rs, None, ALU.mult), w0)
                vstep(lambda e2: e2.tensor_reduce(m1, r1, mybir.AxisListType.X, ALU.max))
                vstep(lambda e2: e2.tensor_scalar(r1, r1, m1, None, ALU.subtract))
                mk.step("scalar", lambda e2: e2.activation(r1, r1, AF.Exp),
                        waits=[(s_dve, s_dve.n)], incs=[(s_act, 1)]); s_act.inc()
                prev_exp_act = (s_act, s_act.n)
                vstep(lambda e2: e2.tensor_reduce(m1, r1, mybir.AxisListType.X, ALU.add),
                      [(s_act, s_act.n)])
                vstep(lambda e2: e2.reciprocal(m2, m1))
                vstep(lambda e2: e2.tensor_scalar(r1, r1, m2, None, ALU.mult))
                vstep(lambda e2: e2.tensor_reduce(m1, r1, mybir.AxisListType.X, ALU.max))
                vstep(lambda e2: e2.tensor_scalar(r2, r1, m1, None, ALU.is_ge))
                vstep(lambda e2: e2.scalar_tensor_tensor(r3, r2, -4.0, r1, ALU.mult, ALU.add))
                vstep(lambda e2: e2.tensor_reduce(m2, r3, mybir.AxisListType.X, ALU.max))
                vstep(lambda e2: e2.tensor_scalar(r2, r1, m2, None, ALU.is_ge))
                vstep(lambda e2: e2.tensor_tensor(r1, r1, r2, ALU.mult))
                tb = bM[1] if tc % 2 == 0 else bM[2]
                tw = tb.acquire()
                mk.step("tensor", lambda e2, ap=tb.ap: e2.matmul(ap[0:8, 0:128], r1, g["I128f"], start=True, stop=True),
                        waits=_merge(tw, [(s_dve, s_dve.n)]), incs=[(s_pe, 1)]); s_pe.inc()
                mk.step("vector", lambda e2, d=w8[:, 128 * tc:128 * (tc + 1)], ap=tb.ap: e2.tensor_copy(d, ap[0:8, 0:128]),
                        waits=[(s_pe, s_pe.n)], incs=[(s_dve, 1)]); s_dve.inc()
                tb.release(s_dve, s_dve.n)
                wtok_done = (s_dve, s_dve.n)
            for n in range(NE):
                tb = bM[n % 3]
                tw = tb.acquire()
                mk.step("tensor", lambda e2, l=g["E8"][:, 128 * n:128 * (n + 1)], ap=tb.ap:
                        e2.matmul(ap, l, w8, start=True, stop=True),
                        waits=_merge(tw, [wtok_done]), incs=[(s_pe, 1)]); s_pe.inc()
                mk.step("vector", lambda e2, d=wbc[:, TS * n:TS * (n + 1)], ap=tb.ap: e2.tensor_copy(d, ap),
                        waits=_merge([(s_pe, s_pe.n)], free_aw if n == 0 else []),
                        incs=[(s_dve, 1)]); s_dve.inc()
                tb.release(s_dve, s_dve.n)
            wbc_done = (s_dve, s_dve.n)
            if _DEBUG_TAPS and s == 2 and tag == "f1":
                for (i_, rr) in enumerate(("s1r", "s2r", "rstd", "brow")):
                    mk.dma("out", lambda e2, d=g["dbg_rows"][0:1, i_ * TS:(i_ + 1) * TS], sb=rows[rr]:
                           e2.dma_start(out=d, in_=sb),
                           waits=[(s_dve, s_dve.n)])
                free_rows = _merge([(s_pe, s_pe.n)], mk.pool_now("out"))
            else:
                free_rows = [(s_pe, s_pe.n)]

            h_all = None
            h_done = []
            for k in range(KC):
                xs = xring[:, (k % 4) * TS:(k % 4 + 1) * TS]
                xw = mk.dma("x", lambda e2, d=xs, sp=src[128 * k:128 * (k + 1), TS * s:TS * (s + 1)]:
                            e2.dma_start(out=d, in_=sp),
                            waits=_merge(free_xring[k % 4]))
                xb = xbr[:, (k % 2) * TS:(k % 2 + 1) * TS]
                mk.step("scalar", lambda e2, d=xb, sp=xs: e2.copy(d, sp),
                        waits=_merge([xw], free_xb[k % 2]), incs=[(s_act, 1)]); s_act.inc()
                cast_done = (s_act, s_act.n)
                free_xring[k % 4] = [cast_done]
                tt = ttf[:, (k % 2) * TS:(k % 2 + 1) * TS]
                hk = hsl[:, k * TS:(k + 1) * TS]
                mk.step("vector", lambda e2, tt=tt, xb=xb: e2.tensor_tensor(tt, xb, abc[:, 0:TS], ALU.mult),
                        waits=_merge([ab_done, cast_done], free_tt[k % 2], free_h if k == 0 else []))
                mk.step("vector", lambda e2, hk=hk, tt=tt: e2.tensor_tensor(hk, tt, abc[:, TS:2 * TS], ALU.add),
                        incs=[(s_dve, 1)]); s_dve.inc()
                free_tt[k % 2] = [(s_dve, s_dve.n)]
                free_xb[k % 2] = [(s_dve, s_dve.n)]
                h_all = (s_dve, s_dve.n)
                h_done.append((s_dve, s_dve.n))

            units = [(n, f) for n in range(NE) for f in range(FCH)] + [(-1, fs) for fs in range(FSCH)]
            for ui, (n, f) in enumerate(units):
                shared = n < 0
                if shared:
                    tg = [_sgu_tile(f, 0, h) for h in range(KC)]
                    tu = [_sgu_tile(f, 1, h) for h in range(KC)]
                else:
                    tg = [_gu_tile(n, f, 0, h) for h in range(KC)]
                    tu = [_gu_tile(n, f, 1, h) for h in range(KC)]
                slgu, wg = mk.fetch(tg + tu)
                slg, slu = slgu[:KC], slgu[KC:]
                wu = wg
                bg = bG[ui % 2]; bu = bU[ui % 2]
                bgw = bg.acquire(); buw = bu.acquire()
                for h in range(KC):
                    hw = [h_done[h]] if (ui == 0 and h_done) else ([h_all] if h == 0 else [])
                    mk.step("tensor", lambda e2, sl=slg[h], hh=h, st=(h == 0), sp=(h == KC - 1), ap=bg.ap:
                            e2.matmul(ap, mk.wtile(sl), hsl[:, hh * TS:(hh + 1) * TS], start=st, stop=sp),
                            waits=_merge(wg, bgw, hw) if h == 0 else _merge(hw))
                g_done = mk.t_inc()
                mk.ws_mark(slg, s_pe, s_pe.n)
                for h in range(KC):
                    mk.step("tensor", lambda e2, sl=slu[h], hh=h, st=(h == 0), sp=(h == KC - 1), ap=bu.ap:
                            e2.matmul(ap, mk.wtile(sl), hsl[:, hh * TS:(hh + 1) * TS], start=st, stop=sp),
                            waits=_merge(wu, buw) if h == 0 else ())
                u_done = mk.t_inc()
                mk.ws_mark(slu, s_pe, s_pe.n)
                ts = tsilu[:, (ui % 2) * TS:(ui % 2 + 1) * TS]
                mk.step("scalar", lambda e2, ts=ts, ap=bg.ap: e2.activation(ts, ap, AF.Silu),
                        waits=_merge([g_done], free_ts[ui % 2]), incs=[(s_act, 1)]); s_act.inc()
                silu_done = (s_act, s_act.n)
                bg.release(s_act, s_act.n)
                if shared:
                    dap = ash[:, f * TS:(f + 1) * TS]
                    mk.step("vector", lambda e2, d=dap, ts=ts, ap=bu.ap: e2.tensor_tensor(d, ts, ap, ALU.mult),
                            waits=_merge([u_done, silu_done], free_aw), incs=[(s_dve, 1)]); s_dve.inc()
                else:
                    dap = aall[:, (n * FCH + f) * TS:(n * FCH + f + 1) * TS]
                    wn = wbc[:, TS * n:TS * (n + 1)]
                    mk.step("vector", lambda e2, d=dap, ts=ts, ap=bu.ap: e2.tensor_tensor(d, ts, ap, ALU.mult),
                            waits=_merge([u_done, silu_done, wbc_done], free_aw), incs=[(s_dve, 1)]); s_dve.inc()
                    mk.step("vector", lambda e2, d=dap, wn=wn: e2.tensor_tensor(d, d, wn, ALU.mult),
                            incs=[(s_dve, 1)]); s_dve.inc()
                free_ts[ui % 2] = [(s_dve, s_dve.n)]
                bu.release(s_dve, s_dve.n)
            a_ready = (s_dve, s_dve.n)
            free_h = [(s_pe, s_pe.n)]
            if _DEBUG_TAPS and s == 2 and tag == "f1":
                for (dname, sb) in [("dbg_h", hsl), ("dbg_l8", l8), ("dbg_w8", w8), ("dbg_wbc", wbc),
                                    ("dbg_a", aall[:, 0:11 * TS]), ("dbg_as", ash[:, 0:4 * TS]),
                                    ("dbg_abc", abc)]:
                    mk.dma("out", lambda e2, d=g[dname], sb=sb: e2.dma_start(out=d, in_=sb),
                           waits=[a_ready])

            dunits = list(range(88)) + list(range(88, 110))
            for h in range(KC):
                bd = bG[h % 2]
                bdw = bd.acquire()
                first = True
                done_groups = 0
                for c0 in range(0, 110, 28):
                    cids = dunits[c0:c0 + 28]
                    sl, wv = mk.fetch([_down_tile(h, j) for j in cids])
                    for idx, j in enumerate(cids):
                        rhs = aall[:, j * TS:(j + 1) * TS] if j < 88 else ash[:, (j - 88) * TS:(j - 87) * TS]
                        mk.step("tensor", lambda e2, sl2=sl[idx], r=rhs, st=first and idx == 0,
                                sp=(c0 + 28 >= 110 and idx == len(cids) - 1), ap=bd.ap:
                                e2.matmul(ap, mk.wtile(sl2), r, start=st, stop=sp),
                                waits=_merge(wv, bdw, [a_ready]) if (first and idx == 0) else (_merge(wv) if idx == 0 else ()))
                    first = False
                    gv = mk.t_inc()
                    mk.ws_mark(sl, s_pe, s_pe.n)
                d_done = (s_pe, s_pe.n)
                xd = xdr[:, (h % 2) * TS:(h % 2 + 1) * TS]
                xw = mk.dma("x", lambda e2, d=xd, sp=src[128 * h:128 * (h + 1), TS * s:TS * (s + 1)]:
                            e2.dma_start(out=d, in_=sp),
                            waits=_merge(free_xdr[h % 2]))
                xo = xout[:, (h % 2) * TS:(h % 2 + 1) * TS]
                mk.step("vector", lambda e2, xo=xo, xd=xd, ap=bd.ap:
                        e2.scalar_tensor_tensor(xo, ap, 0.5, xd, ALU.mult, ALU.add),
                        waits=_merge([d_done, xw], free_xout[h % 2]), incs=[(s_dve, 1)]); s_dve.inc()
                bd.release(s_dve, s_dve.n)
                free_xdr[h % 2] = [(s_dve, s_dve.n)]
                ov = mk.dma("out", lambda e2, d=dst[128 * h:128 * (h + 1), TS * s:TS * (s + 1)], xo=xo:
                            e2.dma_start(out=d, in_=xo),
                            waits=[(s_dve, s_dve.n)])
                free_xout[h % 2] = [ov]
            free_aw = [(s_pe, s_pe.n)]



def _emit_attn(mk, src, dst, bar):
    nc, g = mk.nc, mk.g
    s_pe, s_act, s_dve = mk.s_pe, mk.s_act, mk.s_dve
    s_cck, s_ccv = mk.s_cck, mk.s_ccv
    B = mk.banks
    SC = 1.0 / math.sqrt(HD)
    with ExitStack() as ph:
        e = ph.enter_context
        q_all = e(nc.sbuf_tensor("a_q", [128, NH * TLOC], BF16))[:]
        qdone = {}

        with ExitStack() as p2:
            e2_ = p2.enter_context
            xring = e2_(nc.sbuf_tensor("a_xring", [128, 4 * TS], F32))[:]
            xbr = e2_(nc.sbuf_tensor("a_xbr", [128, 2 * TS], BF16))[:]
            hsl = e2_(nc.sbuf_tensor("a_h", [128, KC * TS], BF16))[:]
            sqr = e2_(nc.sbuf_tensor("a_sq", [128, 2 * TS], F32))[:]
            abc = e2_(nc.sbuf_tensor("a_abc", [128, 2 * TS], F32))[:]
            ttf = e2_(nc.sbuf_tensor("a_ttf", [128, 2 * TS], F32))[:]
            ctab = e2_(nc.sbuf_tensor("a_ct", [128, TLOC], BF16))[:]
            stab = e2_(nc.sbuf_tensor("a_st", [128, TLOC], BF16))[:]
            kst = e2_(nc.sbuf_tensor("a_kst", [128, 2 * TS], BF16))[:]
            krot = e2_(nc.sbuf_tensor("a_krot", [128, 2 * TS], BF16))[:]
            rtmp = e2_(nc.sbuf_tensor("a_rtmp", [128, 2 * TS], BF16))[:]
            vst = e2_(nc.sbuf_tensor("a_vst", [128, 2 * 2048], BF16))[:]
            a2 = e2_(nc.sbuf_tensor("a_a2", [1, TLOC], F32))[:]
            b2 = e2_(nc.sbuf_tensor("a_b2", [1, TLOC], F32))[:]
            rows = {nm: e2_(nc.sbuf_tensor("a_" + nm, [1, TS], F32))[:]
                    for nm in ("s1r", "s2r", "rt1", "rt2", "negm", "rstd", "brow")}

            t1_ = mk.dma("wg", lambda e3: e3.dma_start(out=ctab, in_=g["ctab_d"]), waits=bar)
            t2_ = mk.dma("wg", lambda e3: e3.dma_start(out=stab, in_=g["stab_d"]), waits=bar)
            tab_w = _merge([t1_], [t2_])

            st_ = {"free_xring": [list(bar) for _ in range(4)],
                   "free_xb": [list(bar), list(bar)],
                   "free_sq": [[], []],
                   "free_tt": [[], []],
                   "free_h": list(bar),
                   "free_rows": list(bar),
                   "prev_rstd": None}
            free_kst = [[], []]
            free_krot = [[], []]
            free_vst = [[], []]

            def ln_pass(s, with_stats):
                if with_stats:
                    s1B, s2B = B[0], B[1]
                    s1w, s2w = s1B.acquire(), s2B.acquire()
                    for k in range(KC):
                        xs = xring[:, (k % 4) * TS:(k % 4 + 1) * TS]
                        xw = mk.dma("x", lambda e3, d=xs, sp=src[128 * k:128 * (k + 1), TS * s:TS * (s + 1)]:
                                    e3.dma_start(out=d, in_=sp),
                                    waits=_merge(st_["free_xring"][k % 4]))
                        sq = sqr[:, (k % 2) * TS:(k % 2 + 1) * TS]
                        mk.step("vector", lambda e3, d=sq, a=xs: e3.tensor_tensor(d, a, a, ALU.mult),
                                waits=_merge([xw], st_["free_sq"][k % 2]), incs=[(s_dve, 1)]); s_dve.inc()
                        sq_done = (s_dve, s_dve.n)
                        mk.step("tensor", lambda e3, a=xs, stt=(k == 0), spp=(k == KC - 1):
                                e3.matmul(s1B.ap[0:8, :], g["onescol8"], a, start=stt, stop=spp),
                                waits=_merge([xw], s1w if k == 0 else []))
                        mk.step("tensor", lambda e3, a=sq, stt=(k == 0), spp=(k == KC - 1):
                                e3.matmul(s2B.ap[0:8, :], g["onescol8"], a, start=stt, stop=spp),
                                waits=_merge([sq_done], s2w if k == 0 else []), incs=[(s_pe, 1)]); s_pe.inc()
                        st_["free_xring"][k % 4] = [(s_pe, s_pe.n)]
                        st_["free_sq"][k % 2] = [(s_pe, s_pe.n)]
                    stats_pe = (s_pe, s_pe.n)
                    mk.step("vector", lambda e3: e3.tensor_copy(rows["s1r"], s1B.ap[0:1, :]),
                            waits=_merge([stats_pe], st_["free_rows"]), incs=[(s_dve, 1)]); s_dve.inc()
                    mk.step("vector", lambda e3: e3.tensor_copy(rows["s2r"], s2B.ap[0:1, :]),
                            incs=[(s_dve, 1)]); s_dve.inc()
                    s1B.release(s_dve, s_dve.n)
                    s2B.release(s_dve, s_dve.n)
                    rowm, rstd_act = _ln_rows(mk, rows, st_["prev_rstd"])
                    st_["prev_rstd"] = rstd_act
                    mk.step("vector", lambda e3, s=s: e3.tensor_copy(a2[0:1, TS * s:TS * (s + 1)], rows["rstd"]))
                    mk.step("vector", lambda e3, s=s: e3.tensor_copy(b2[0:1, TS * s:TS * (s + 1)], rows["brow"]),
                            incs=[(s_dve, 1)]); s_dve.inc()
                    rowm = (s_dve, s_dve.n)
                    asrc, bsrc = rows["rstd"], rows["brow"]
                    st_["free_rows"] = []
                else:
                    rowm = (s_dve, s_dve.n)
                    asrc = a2[0:1, TS * s:TS * (s + 1)]
                    bsrc = b2[0:1, TS * s:TS * (s + 1)]
                ab_done = _bcast_rows(mk, B[0], B[1], asrc, bsrc, abc, rowm)
                h_all = None
                h_done.clear()
                for k in range(KC):
                    xs = xring[:, (k % 4) * TS:(k % 4 + 1) * TS]
                    xw = mk.dma("x", lambda e3, d=xs, sp=src[128 * k:128 * (k + 1), TS * s:TS * (s + 1)]:
                                e3.dma_start(out=d, in_=sp),
                                waits=_merge(st_["free_xring"][k % 4]))
                    xb = xbr[:, (k % 2) * TS:(k % 2 + 1) * TS]
                    mk.step("scalar", lambda e3, d=xb, sp=xs: e3.copy(d, sp),
                            waits=_merge([xw], st_["free_xb"][k % 2]), incs=[(s_act, 1)]); s_act.inc()
                    cast_done = (s_act, s_act.n)
                    st_["free_xring"][k % 4] = [cast_done]
                    tt = ttf[:, (k % 2) * TS:(k % 2 + 1) * TS]
                    hk = hsl[:, k * TS:(k + 1) * TS]
                    mk.step("vector", lambda e3, tt=tt, xb=xb: e3.tensor_tensor(tt, xb, abc[:, 0:TS], ALU.mult),
                            waits=_merge([ab_done, cast_done], st_["free_tt"][k % 2],
                                         st_["free_h"] if k == 0 else []))
                    mk.step("vector", lambda e3, hk=hk, tt=tt: e3.tensor_tensor(hk, tt, abc[:, TS:2 * TS], ALU.add),
                            incs=[(s_dve, 1)]); s_dve.inc()
                    st_["free_tt"][k % 2] = [(s_dve, s_dve.n)]
                    st_["free_xb"][k % 2] = [(s_dve, s_dve.n)]
                    h_all = (s_dve, s_dve.n)
                    h_done.append(h_all)
                return h_all

            def rope(src_ap, dst_ap, cs, ss, extra_w):
                t1 = rtmp[0:64, 0:TS]; t2 = rtmp[0:64, TS:2 * TS]
                t3 = rtmp[64:128, 0:TS]; t4 = rtmp[64:128, TS:2 * TS]
                mk.step("vector", lambda e3: e3.tensor_tensor(t1, src_ap[0:64, :], cs[0:64, :], ALU.mult),
                        waits=_merge(extra_w, tab_w))
                mk.step("vector", lambda e3: e3.tensor_tensor(t2, src_ap[64:128, :], ss[64:128, :], ALU.mult))
                mk.step("vector", lambda e3: e3.tensor_tensor(t3, src_ap[64:128, :], cs[64:128, :], ALU.mult))
                mk.step("vector", lambda e3: e3.tensor_tensor(t4, src_ap[0:64, :], ss[0:64, :], ALU.mult))
                mk.step("vector", lambda e3: e3.tensor_tensor(dst_ap[0:64, :], t1, t2, ALU.subtract))
                mk.step("vector", lambda e3: e3.tensor_tensor(dst_ap[64:128, :], t3, t4, ALU.add),
                        incs=[(s_dve, 1)]); s_dve.inc()
                return (s_dve, s_dve.n)

            bQ = [B[2], B[3]]
            h_done = []
            k_store_w = []
            for s in range(NSL):
                h_all = ln_pass(s, with_stats=True)
                cs = ctab[0:128, TS * s:TS * (s + 1)]
                ss = stab[0:128, TS * s:TS * (s + 1)]
                for oc in range(NH):
                    sl, wv = mk.fetch([_qk_tile(oc, h) for h in range(KC)])
                    bq = bQ[oc % 2]
                    bqw = bq.acquire()
                    for h in range(KC):
                        hw = [h_done[h]] if (oc == 0 and h_done) else ([h_all] if h == 0 else [])
                        mk.step("tensor", lambda e3, sl2=sl[h], hh=h, stt=(h == 0), spp=(h == KC - 1), ap=bq.ap:
                                e3.matmul(ap, mk.wtile(sl2), hsl[:, hh * TS:(hh + 1) * TS], start=stt, stop=spp),
                                waits=_merge(wv, bqw, hw) if h == 0 else _merge(hw))
                    kd = mk.t_inc()
                    mk.ws_mark(sl, s_pe, s_pe.n)
                    kb = kst[:, (oc % 2) * TS:(oc % 2 + 1) * TS]
                    mk.step("scalar", lambda e3, d=kb, ap=bq.ap: e3.copy(d, ap),
                            waits=_merge([kd], free_kst[oc % 2]), incs=[(s_act, 1)]); s_act.inc()
                    bq.release(s_act, s_act.n)
                    kr = krot[:, (oc % 2) * TS:(oc % 2 + 1) * TS]
                    rd = rope(kb, kr, cs, ss, _merge([(s_act, s_act.n)], free_krot[oc % 2]))
                    free_kst[oc % 2] = [rd]
                    kv_ = mk.dma("kv", lambda e3, d=g["k_own"][:][128 * oc:128 * (oc + 1), TS * s:TS * (s + 1)], kr=kr:
                                 e3.dma_start(out=d, in_=kr),
                                 waits=[rd])
                    k_store_w.append(kv_)
                    free_krot[oc % 2] = [kv_]
                for p in range(2):
                    rA = mk.region(4, 8)
                    rB = mk.region(0, 4)
                    rAw = rA.acquire(); rBw = rB.acquire()
                    for d in range(KC):
                        sl, wv = mk.fetch([_v_tile(d, vc) for vc in range(16)], align16=True)
                        for tt_i in range(2):
                            ti = 2 * p + tt_i
                            lhsT = hsl[:, d * TS + 128 * ti:d * TS + 128 * (ti + 1)]
                            reg = rA if tt_i == 0 else rB
                            for vs in range(4):
                                r0 = sl[4 * vs]
                                rhs = mk.wtile(r0, 4)
                                extra = _merge(wv, rAw, rBw, [h_all]) if (d == 0 and tt_i == 0 and vs == 0) \
                                    else (_merge(wv) if (tt_i == 0 and vs == 0) else ())
                                mk.step("tensor", lambda e3, l=lhsT, r=rhs, stt=(d == 0), spp=(d == KC - 1),
                                        ap=reg.ap, vs=vs:
                                        e3.matmul(ap[:, 512 * vs:512 * (vs + 1)], l, r, start=stt, stop=spp),
                                        waits=extra)
                        gv = mk.t_inc()
                        mk.ws_mark(sl, s_pe, s_pe.n)
                    v_done = (s_pe, s_pe.n)
                    for tt_i in range(2):
                        reg = rA if tt_i == 0 else rB
                        vsb = vst[:, tt_i * 2048:(tt_i + 1) * 2048]
                        mk.step("vector", lambda e3, d=vsb, ap=reg.ap: e3.tensor_copy(d, ap),
                                waits=_merge([v_done], free_vst[tt_i]), incs=[(s_dve, 1)]); s_dve.inc()
                        reg.release(s_dve, s_dve.n)
                        ti = 4 * s + 2 * p + tt_i
                        kv_ = mk.dma("kv", lambda e3, d=g["v_own"][:][128 * ti:128 * (ti + 1), :], vsb=vsb:
                                     e3.dma_start(out=d, in_=vsb),
                                     waits=[(s_dve, s_dve.n)])
                        free_vst[tt_i] = [kv_]
                st_["free_h"] = [(s_pe, s_pe.n)]

            mk.step("gpsimd", lambda e3: e3.collective_compute(
                "AllGather", ALU.bypass, replica_groups=[[0, 1], [2, 3], [4, 5], [6, 7]],
                ins=[g["k_own"].ap().opt()], outs=[g["k_full"].ap().opt()]),
                waits=_merge(*[[w] for w in k_store_w]), incs=[(s_cck, 1)]); s_cck.inc()
            mk.step("gpsimd", lambda e3: e3.collective_compute(
                "AllGather", ALU.bypass, replica_groups=[[0, 1], [2, 3], [4, 5], [6, 7]],
                ins=[g["v_own"].ap().opt()], outs=[g["v_full"].ap().opt()]),
                waits=mk.pool_now("kv"), incs=[(s_ccv, 1)]); s_ccv.inc()

            for s in range(NSL):
                h_all = ln_pass(s, with_stats=False)
                cs = ctab[0:128, TS * s:TS * (s + 1)]
                ss = stab[0:128, TS * s:TS * (s + 1)]
                for oc in range(NH):
                    sl, wv = mk.fetch([_qk_tile(16 + oc, h) for h in range(KC)])
                    bq = bQ[oc % 2]
                    bqw = bq.acquire()
                    for h in range(KC):
                        hw = [h_done[h]] if (oc == 0 and h_done) else ([h_all] if h == 0 else [])
                        mk.step("tensor", lambda e3, sl2=sl[h], hh=h, stt=(h == 0), spp=(h == KC - 1), ap=bq.ap:
                                e3.matmul(ap, mk.wtile(sl2), hsl[:, hh * TS:(hh + 1) * TS], start=stt, stop=spp),
                                waits=_merge(wv, bqw, hw) if h == 0 else _merge(hw))
                    kd = mk.t_inc()
                    mk.ws_mark(sl, s_pe, s_pe.n)
                    kb = kst[:, (oc % 2) * TS:(oc % 2 + 1) * TS]
                    mk.step("scalar", lambda e3, d=kb, ap=bq.ap: e3.copy(d, ap),
                            waits=_merge([kd], free_kst[oc % 2]), incs=[(s_act, 1)]); s_act.inc()
                    bq.release(s_act, s_act.n)
                    qd = q_all[:, oc * TLOC + TS * s:oc * TLOC + TS * (s + 1)]
                    rd = rope(kb, qd, cs, ss, [(s_act, s_act.n)])
                    free_kst[oc % 2] = [rd]
                    qdone[(oc, s)] = rd
                st_["free_h"] = [(s_pe, s_pe.n)]
            p2a_done = _bar(mk)

        with ExitStack() as p3:
            e3_ = p3.enter_context
            attn = e3_(nc.sbuf_tensor("a_attn", [128, NH * TLOC], BF16))[:]
            attn_all = None
            with ExitStack() as pf:
                e4_ = pf.enter_context
                kh = e4_(nc.sbuf_tensor("a_kh", [128, 2 * 4096], BF16))[:]
                vt = e4_(nc.sbuf_tensor("a_vt", [128, 2 * 4096], BF16))[:]
                es = e4_(nc.sbuf_tensor("a_es", [128, 3 * 1024], BF16))[:]
                rbc = e4_(nc.sbuf_tensor("a_rbc", [128, 2 * TS], F32))[:]
                rcp = e4_(nc.sbuf_tensor("a_rcp", [1, 2 * TS], F32))[:]
                kf = g["k_full"].ap()
                vf = g["v_full"].ap()
                bST = [mk.region(0, 2), mk.region(2, 4)]
                bAV = [B[4], B[5]]
                bDN = [B[6], B[7]]
                free_kh = [list(p2a_done), list(p2a_done)]
                free_vt = [list(p2a_done), list(p2a_done)]
                free_es = [list(p2a_done) for _ in range(3)]
                free_rbc = [list(p2a_done), list(p2a_done)]
                khl = {}

                def load_head(hh):
                    buf = hh % 2
                    w = _merge(free_kh[buf], [(s_cck, 1)])
                    ws_ = []
                    for r in range(2):
                        dd = kh[:, buf * 4096 + r * 2048:buf * 4096 + (r + 1) * 2048]
                        sp = kf[r * D + 128 * hh:r * D + 128 * (hh + 1), :]
                        ws_.append(mk.dma("khl", lambda e5, d=dd, s2=sp: e5.dma_start(out=d, in_=s2),
                                          waits=w))
                    w2 = _merge(free_vt[buf], [(s_ccv, 1)])
                    for r in range(2):
                        dd = vt[:, buf * 4096 + r * 2048:buf * 4096 + (r + 1) * 2048].rearrange(
                            "p (m c) -> p m c", c=128)
                        sp = vf[r * TLOC:(r + 1) * TLOC, 128 * hh:128 * (hh + 1)].rearrange(
                            "(m p) c -> p m c", p=128)
                        ws_.append(mk.dma("khl", lambda e5, d=dd, s2=sp: e5.dma_start(out=d, in_=s2),
                                          waits=w2))
                    khl[hh] = _merge(*[[x] for x in ws_])

                load_head(0)
                uu = 0
                for hh in range(NH):
                    if hh + 1 < NH:
                        load_head(hh + 1)
                    buf = hh % 2
                    for s in range(NSL):
                        n_k = 8 * s + 8
                        bav = bAV[uu % 2]; bdn = bDN[uu % 2]
                        avw = bav.acquire(); dnw = bdn.acquire()
                        qw = qdone[(hh, s)]
                        rhs_q = q_all[:, hh * TLOC + TS * s:hh * TLOC + TS * (s + 1)]
                        pend_av = [None]

                        def emit_scores(jp):
                            bst = bST[jp % 2]
                            stw = bst.acquire()
                            for jj, j in enumerate((2 * jp, 2 * jp + 1)):
                                r, m = j % 2, j // 2
                                lhsT = kh[:, buf * 4096 + r * 2048 + 128 * m:buf * 4096 + r * 2048 + 128 * (m + 1)]
                                e_ = j - 8 * s
                                masks = []
                                if 0 <= e_ // 2 < 4 and e_ >= 0:
                                    masks.append((e_ // 2, g["mp0"] if e_ % 2 == 0 else g["mp1"]))
                                for qq in range(4):
                                    if e_ > 2 * qq + 1:
                                        masks.append((qq, g["mneg"]))
                                mk.step("tensor", lambda e5, l=lhsT, jj=jj, ap=bst.ap, nm=bool(masks), r2=rhs_q:
                                        e5.matmul(ap[:, 512 * jj:512 * (jj + 1)], l, r2, start=True, stop=not nm),
                                        waits=_merge(stw, khl[hh], [qw]) if jj == 0 else ())
                                for mi, (qi, msk) in enumerate(masks):
                                    mk.step("tensor", lambda e5, m2=msk, jj=jj, qi=qi, ap=bst.ap,
                                            sp=(mi == len(masks) - 1):
                                            e5.matmul(ap[:, 512 * jj + 128 * qi:512 * jj + 128 * (qi + 1)],
                                                      g["I128b"], m2, start=False, stop=sp))
                            sdone = mk.t_inc()
                            esb = es[:, (jp % 3) * 1024:(jp % 3 + 1) * 1024]
                            mk.step("scalar", lambda e5, d=esb, ap=bst.ap: e5.activation(d, ap, AF.Exp, scale=SC),
                                    waits=_merge([sdone], free_es[jp % 3]), incs=[(s_act, 1)]); s_act.inc()
                            bst.release(s_act, s_act.n)
                            return (jp, esb, (s_act, s_act.n))

                        def emit_av(state):
                            jp, esb, e_done = state
                            for jj, j in enumerate((2 * jp, 2 * jp + 1)):
                                r, m = j % 2, j // 2
                                esl = esb[:, 512 * jj:512 * (jj + 1)]
                                mk.step("tensor", lambda e5, esl=esl, stt=(j == 0), spp=(j == n_k - 1), ap=bdn.ap:
                                        e5.matmul(ap[0:8, :], g["onesb"], esl, start=stt, stop=spp),
                                        waits=_merge([e_done], dnw if j == 0 else []) if jj == 0 else ())
                                lhsT = vt[:, buf * 4096 + r * 2048 + 128 * m:buf * 4096 + r * 2048 + 128 * (m + 1)]
                                mk.step("tensor", lambda e5, l=lhsT, esl=esl, stt=(j == 0), spp=(j == n_k - 1), ap=bav.ap:
                                        e5.matmul(ap, l, esl, start=stt, stop=spp),
                                        waits=_merge(avw, khl[hh]) if j == 0 else ())
                            fe = mk.t_inc()
                            free_es[jp % 3] = [fe]

                        for jp in range(n_k // 2):
                            st8 = emit_scores(jp)
                            if pend_av[0] is not None:
                                emit_av(pend_av[0])
                            pend_av[0] = st8
                        emit_av(pend_av[0])
                        pend_av[0] = None
                        unit_pe = (s_pe, s_pe.n)
                        rc = rcp[0:1, (uu % 2) * TS:(uu % 2 + 1) * TS]
                        mk.step("vector", lambda e5, rc=rc, ap=bdn.ap: e5.reciprocal(rc, ap[0:1, :]),
                                waits=_merge([unit_pe], free_rbc[uu % 2]), incs=[(s_dve, 1)]); s_dve.inc()
                        bdn.release(s_dve, s_dve.n)
                        bb = bST[0]
                        bbw = bb.acquire()
                        mk.step("tensor", lambda e5, rc=rc, ap=bb.ap:
                                e5.matmul(ap[:, 0:TS], g["onesrow"], rc, start=True, stop=True),
                                waits=_merge(bbw, [(s_dve, s_dve.n)]), incs=[(s_pe, 1)]); s_pe.inc()
                        rb = rbc[:, (uu % 2) * TS:(uu % 2 + 1) * TS]
                        mk.step("vector", lambda e5, rb=rb, ap=bb.ap: e5.tensor_copy(rb, ap[:, 0:TS]),
                                waits=[(s_pe, s_pe.n)], incs=[(s_dve, 1)]); s_dve.inc()
                        bb.release(s_dve, s_dve.n)
                        dd = attn[:, hh * TLOC + TS * s:hh * TLOC + TS * (s + 1)]
                        mk.step("vector", lambda e5, d=dd, rb=rb, ap=bav.ap:
                                e5.scalar_tensor_tensor(d, ap, 1.0, rb, ALU.mult, ALU.mult),
                                incs=[(s_dve, 1)]); s_dve.inc()
                        bav.release(s_dve, s_dve.n)
                        free_rbc[uu % 2] = [(s_dve, s_dve.n)]
                        uu += 1
                    free_kh[buf] = [(s_pe, s_pe.n)]
                    free_vt[buf] = [(s_pe, s_pe.n)]
                attn_all = (s_dve, s_dve.n)
                fl_done = _bar(mk)

            with ExitStack() as p4:
                e5_ = p4.enter_context
                x1r = e5_(nc.sbuf_tensor("a_x1r", [128, 2 * TS], F32))[:]
                x2o = e5_(nc.sbuf_tensor("a_x2o", [128, 2 * TS], F32))[:]
                bO = [B[0], B[1]]
                free_x1r = [list(fl_done), list(fl_done)]
                free_x2o = [[], []]
                for s in range(NSL):
                    for oc in range(KC):
                        sl, wv = mk.fetch([_wo_tile(oc, k) for k in range(KC)])
                        bo = bO[oc % 2]
                        bow = bo.acquire()
                        for k in range(KC):
                            rhs = attn[:, k * TLOC + TS * s:k * TLOC + TS * (s + 1)]
                            mk.step("tensor", lambda e5, sl2=sl[k], r=rhs, stt=(k == 0), spp=(k == KC - 1), ap=bo.ap:
                                    e5.matmul(ap, mk.wtile(sl2), r, start=stt, stop=spp),
                                    waits=_merge(wv, bow, [attn_all]) if k == 0 else ())
                        od = mk.t_inc()
                        mk.ws_mark(sl, s_pe, s_pe.n)
                        xd = x1r[:, (oc % 2) * TS:(oc % 2 + 1) * TS]
                        xw = mk.dma("x", lambda e5, d=xd, sp=src[128 * oc:128 * (oc + 1), TS * s:TS * (s + 1)]:
                                    e5.dma_start(out=d, in_=sp),
                                    waits=_merge(free_x1r[oc % 2]))
                        xo = x2o[:, (oc % 2) * TS:(oc % 2 + 1) * TS]
                        mk.step("vector", lambda e5, xo=xo, xd=xd, ap=bo.ap:
                                e5.scalar_tensor_tensor(xo, ap, 1.0, xd, ALU.mult, ALU.add),
                                waits=_merge([od, xw], free_x2o[oc % 2]), incs=[(s_dve, 1)]); s_dve.inc()
                        bo.release(s_dve, s_dve.n)
                        free_x1r[oc % 2] = [(s_dve, s_dve.n)]
                        ov = mk.dma("out", lambda e5, d=dst[128 * oc:128 * (oc + 1), TS * s:TS * (s + 1)], xo=xo:
                                    e5.dma_start(out=d, in_=xo),
                                    waits=[(s_dve, s_dve.n)])
                        free_x2o[oc % 2] = [ov]



def _build_wblob(w_qkv, w_o, gate_w, up_w, down_w, sh_g, sh_u, sh_d):
    tiles = np.empty((NTILES, 128, 128), dtype=np.float32)
    G = gate_w.transpose(0, 2, 1).reshape(NE, KC, 128, FCH, 128)
    U = up_w.transpose(0, 2, 1).reshape(NE, KC, 128, FCH, 128)
    gu = np.stack([G.transpose(0, 3, 1, 2, 4), U.transpose(0, 3, 1, 2, 4)], axis=2)
    tiles[0:N_GU] = gu.reshape(N_GU, 128, 128)
    SG = sh_g.T.reshape(KC, 128, FSCH, 128).transpose(2, 0, 1, 3)
    SU = sh_u.T.reshape(KC, 128, FSCH, 128).transpose(2, 0, 1, 3)
    tiles[N_GU:N_GU + N_SGU] = np.stack([SG, SU], axis=1).reshape(N_SGU, 128, 128)
    DT = down_w.transpose(0, 2, 1).reshape(NE, FCH, 128, KC, 128).transpose(3, 0, 1, 2, 4)
    SD = sh_d.T.reshape(FSCH, 128, KC, 128).transpose(2, 0, 1, 3)
    dn = np.concatenate([DT.reshape(KC, NE * FCH, 128, 128), SD], axis=1)
    tiles[N_GU + N_SGU:SEG_QK0] = dn.reshape(N_DOWN, 128, 128)
    WQ = w_qkv.reshape(KC, 128, 48, 128).transpose(2, 0, 1, 3)
    tiles[SEG_QK0:SEG_QK0 + 512] = WQ[np.r_[16:32, 0:16]].reshape(512, 128, 128)
    tiles[SEG_V0:SEG_V0 + 256] = WQ[32:48].transpose(1, 0, 2, 3).reshape(256, 128, 128)
    tiles[SEG_WO0:] = w_o.reshape(KC, 128, KC, 128).transpose(2, 0, 1, 3).reshape(256, 128, 128)
    return np.ascontiguousarray(tiles.transpose(1, 0, 2).reshape(128, NTILES * 128))


def _host_inputs(inputs):
    x = np.asarray(inputs["L0_attn_x"])
    wblob = _build_wblob(
        np.asarray(inputs["L0_attn_w_qkv"]), np.asarray(inputs["L0_attn_w_o"]),
        np.asarray(inputs["L0_ffn_gate_w"]), np.asarray(inputs["L0_ffn_up_w"]),
        np.asarray(inputs["L0_ffn_down_w"]), np.asarray(inputs["L0_ffn_sh_gate_w"]),
        np.asarray(inputs["L0_ffn_sh_up_w"]), np.asarray(inputs["L0_ffn_sh_down_w"]))
    gwpm = np.ascontiguousarray(
        np.asarray(inputs["L0_ffn_gate_weight"]).T.reshape(KC, 128, NE).transpose(1, 0, 2).reshape(128, KC * NE))
    consts = np.zeros((128, 1664), dtype=np.float32)
    consts[:, 1536:1664] = NEG
    for n in range(NE):
        consts[n, 128 * n:128 * (n + 1)] = 1.0
    consts[np.arange(8), 1024 + np.arange(8)] = 1.0
    consts[np.arange(128), 1152 + np.arange(128)] = 1.0
    consts[:, 1424:1441] = 1.0
    consts[0, 1281:1409] = 1.0
    tri = np.where(np.arange(128)[:, None] <= np.arange(128)[None, :], 0.0, NEG).astype(np.float32)
    inv_freq = 1.0 / 10000.0 ** (np.arange(0, HD, 2, dtype=np.float64) / HD)
    per_core = []
    for c in range(8):
        b, p = c // 2, c % 2
        i = np.arange(TLOC)
        pos = 256 * (i // 128) + 128 * p + (i % 128)
        xc = np.ascontiguousarray(x[b][pos, :].T)
        fr = pos[None, :] * inv_freq[:, None]
        per_core.append({
            "x_in": xc, "wblob": wblob, "gwpm": gwpm, "consts": consts,
            "mask0": np.ascontiguousarray(tri if p == 0 else np.zeros((128, 128), np.float32)),
            "mask1": np.ascontiguousarray(np.full((128, 128), NEG, np.float32) if p == 0 else tri),
            "ctab": np.ascontiguousarray(np.tile(np.cos(fr).astype(np.float32), (2, 1))),
            "stab": np.ascontiguousarray(np.tile(np.sin(fr).astype(np.float32), (2, 1))),
        })
    return per_core


def _assemble(results):
    y = np.empty((4, 4096, D), dtype=np.float32)
    for c in range(8):
        b, p = c // 2, c % 2
        i = np.arange(TLOC)
        pos = 256 * (i // 128) + 128 * p + (i % 128)
        y[b][pos, :] = results[c]["out_x"].T
    return y


def kernel(**inputs):
    nc = build_program(_PHASES)
    in_maps = _host_inputs(inputs)
    res = run_bass_kernel_spmd(nc, in_maps, core_ids=list(range(8)))
    return _assemble(res.results)


# version 06 (final)
# speedup vs baseline: 1.0853x; 1.0853x over previous
import math
from contextlib import ExitStack

import numpy as np

import concourse.bass as bass
import concourse.mybir as mybir
from concourse.bass_utils import run_bass_kernel_spmd

F32 = mybir.dt.float32
BF16 = mybir.dt.bfloat16
AF = mybir.ActivationFunctionType
ALU = mybir.AluOpType

D = 2048
KC = 16
TLOC = 2048
TS = 512
NSL = 4
NH = 16
HD = 128
NE = 8
FCH = 11
FSCH = 22
NEG = -1.0e6

N_GU = NE * FCH * 2 * KC
N_SGU = FSCH * 2 * KC
N_DOWN = KC * (NE * FCH + FSCH)
SEG_QK0 = N_GU + N_SGU + N_DOWN
SEG_V0 = SEG_QK0 + 32 * KC
SEG_WO0 = SEG_V0 + KC * 16
NTILES = SEG_WO0 + 16 * KC

RING = 80

_PHASES = ("ffn1", "attn", "ffn3")
_DETECT_RACES = False
_DEBUG_TAPS = False


def _gu_tile(n, f, which, h):
    return (n * FCH + f) * 32 + which * 16 + h


def _sgu_tile(fs, which, h):
    return N_GU + fs * 32 + which * 16 + h


def _down_tile(h, j):
    return N_GU + N_SGU + h * (NE * FCH + FSCH) + j


def _qk_tile(oc, h):
    return SEG_QK0 + oc * 16 + h


def _v_tile(d, vc):
    return SEG_V0 + d * 16 + vc


def _wo_tile(oc, k):
    return SEG_WO0 + oc * 16 + k


class Sem:
    __slots__ = ("h", "n", "name")

    def __init__(self, mk, name):
        self.h = mk.ctx.enter_context(mk.nc.semaphore(name))
        self.n = 0
        self.name = name

    def inc(self, amt=1):
        self.n += amt
        return self.n


class Bank:

    def __init__(self, ap, parts):
        self.ap = ap
        self.parts = parts

    def acquire(self):
        w = []
        for p in self.parts:
            w.extend(p.free_at)
            p.free_at = []
        return w

    def release(self, sem, val):
        for p in self.parts:
            p.free_at = [(sem, val)]


def _merge(*wait_lists):
    best = {}
    for wl in wait_lists:
        for (sem, val) in wl:
            if val > best.get(sem, -1):
                best[sem] = val
    return [(s, v) for s, v in best.items()]


class MK:
    def __init__(self, nc, ctx):
        self.nc = nc
        self.ctx = ctx
        self.plan = []
        for nm in ("s_pe", "s_act", "s_dve", "s_cck", "s_ccv"):
            setattr(self, nm, Sem(self, nm))
        self.pools = {}
        for (nm, queue, n) in [("w", "sync", 8), ("wg", "gpsimd", 4), ("tee", "sync", 4),
                               ("x", "scalar", 6), ("kv", "scalar", 4), ("khl", "scalar", 8),
                               ("out", "scalar", 4)]:
            self.pools[nm] = (queue, [Sem(self, f"p_{nm}{i}") for i in range(n)], [0])
        self.ring_pos = 0
        self.ring_free = [[] for _ in range(RING)]
        self.ring_extra = {}
        self.first_use = np.zeros(NTILES, dtype=bool)
        self.tee_val = [None] * NTILES
        self.pend = []
        self.g = {}
        self.pool = ctx.enter_context(nc.psum_tensor("pool", [128, 4096], F32))[:]
        self.banks = [Bank(self.pool[:, 512 * i:512 * (i + 1)], None) for i in range(8)]
        for b in self.banks:
            b.parts = [b]
            b.free_at = []

    def region(self, lo, hi):
        return Bank(self.pool[:, 512 * lo:512 * hi], [self.banks[i] for i in range(lo, hi)])

    def step(self, eng, fn, waits=(), incs=()):
        for (sem, val) in waits:
            assert val <= sem.n, f"wait {sem.name}={val} > emitted {sem.n}"
        self.plan.append((eng, fn, tuple(waits), tuple(incs)))

    def dma(self, pool, fn, waits=()):
        queue, sems, idx = self.pools[pool]
        s = sems[idx[0] % len(sems)]
        idx[0] += 1
        self.step(queue, fn, waits=_merge(list(waits), [(s, s.n)]), incs=[(s, 16)])
        return (s, s.inc(16))

    def pool_now(self, pool):
        return [(s, s.n) for s in self.pools[pool][1] if s.n > 0]

    def t_inc(self):
        eng, fn, waits, incs = self.plan[-1]
        assert eng == "tensor", eng
        self.plan[-1] = (eng, fn, waits, incs + ((self.s_pe, 1),))
        return (self.s_pe, self.s_pe.inc())

    def fetch(self, tile_ids, align16=False):
        assert len(tile_ids) <= RING - 16
        if align16 and (self.ring_pos % 16) != 0:
            self.flush()
            self.ring_pos += 16 - self.ring_pos % 16
        slots = []
        for t in tile_ids:
            self.pend.append((t, self.ring_pos % RING))
            slots.append(self.ring_pos % RING)
            self.ring_pos += 1
        wv = self.flush()
        return slots, wv

    def flush(self):
        if not self.pend:
            return []
        pend, self.pend = self.pend, []
        i = 0
        out = []
        while i < len(pend):
            j = i + 1
            t0 = pend[i][0]
            while (j < len(pend)
                   and pend[j][0] == pend[j - 1][0] + 1
                   and pend[j][1] == pend[j - 1][1] + 1
                   and bool(self.first_use[pend[j][0]]) == bool(self.first_use[t0])):
                j += 1
            out.append(self._emit_run(pend[i:j]))
            i = j
        return _merge(*[[w] for w in out])

    def _emit_run(self, run):
        t0, sl0 = run[0]
        n = len(run)
        is_first = not self.first_use[t0]
        waits = _merge(*[self.ring_free[sl0 + k] for k in range(n)])
        dst = self.g["wring"][:, sl0 * 128:(sl0 + n) * 128]
        if is_first:
            src = self.g["wblob"][:, t0 * 128:(t0 + n) * 128]
            wv = self.dma("wg", lambda e, d=dst, s=src: e.dma_start(out=d, in_=s), waits=waits)
            tdst = self.g["wbf16"][:, t0 * 128:(t0 + n) * 128]
            tv = self.dma("tee", lambda e, d=tdst, s=dst: e.dma_start(out=d, in_=s), waits=[wv])
            for (t, sl) in run:
                self.first_use[t] = True
                self.tee_val[t] = tv
                self.ring_extra[sl] = tv
            return wv
        else:
            src = self.g["wbf16"][:, t0 * 128:(t0 + n) * 128]
            waits = _merge(waits, *[[self.tee_val[t]] for t, _ in run])
            return self.dma("w", lambda e, d=dst, s=src: e.dma_start(out=d, in_=s), waits=waits)

    def ws_mark(self, slots, sem, val):
        for sl in slots:
            ex = self.ring_extra.pop(sl, None)
            self.ring_free[sl] = [(sem, val)] + ([ex] if ex else [])

    def wtile(self, slot, n=1):
        return self.g["wring"][:, slot * 128:(slot + n) * 128]



def build_program(phases=_PHASES):
    nc = bass.Bass("TRN2", target_bir_lowering=False, detect_race_conditions=_DETECT_RACES)
    ctx = ExitStack()
    mk = MK(nc, ctx)
    g = mk.g

    g["x_in"] = nc.dram_tensor("x_in", [D, TLOC], F32, kind="ExternalInput").ap()
    g["wblob"] = nc.dram_tensor("wblob", [128, NTILES * 128], F32, kind="ExternalInput").ap()
    g["gwpm"] = nc.dram_tensor("gwpm", [128, 128], F32, kind="ExternalInput").ap()
    g["consts"] = nc.dram_tensor("consts", [128, 1664], F32, kind="ExternalInput").ap()
    g["mask0"] = nc.dram_tensor("mask0", [128, 128], F32, kind="ExternalInput").ap()
    g["mask1"] = nc.dram_tensor("mask1", [128, 128], F32, kind="ExternalInput").ap()
    g["ctab_d"] = nc.dram_tensor("ctab", [128, TLOC], F32, kind="ExternalInput").ap()
    g["stab_d"] = nc.dram_tensor("stab", [128, TLOC], F32, kind="ExternalInput").ap()
    g["out"] = nc.dram_tensor("out_x", [D, TLOC], F32, kind="ExternalOutput").ap()
    if _DEBUG_TAPS:
        g["dbg_h"] = nc.dram_tensor("dbg_h", [128, KC * TS], BF16, kind="ExternalOutput").ap()
        g["dbg_l8"] = nc.dram_tensor("dbg_l8", [8, TS], F32, kind="ExternalOutput").ap()
        g["dbg_w8"] = nc.dram_tensor("dbg_w8", [8, TS], F32, kind="ExternalOutput").ap()
        g["dbg_wbc"] = nc.dram_tensor("dbg_wbc", [128, NE * TS], BF16, kind="ExternalOutput").ap()
        g["dbg_a"] = nc.dram_tensor("dbg_a", [128, 11 * TS], BF16, kind="ExternalOutput").ap()
        g["dbg_as"] = nc.dram_tensor("dbg_as", [128, 4 * TS], BF16, kind="ExternalOutput").ap()
        g["dbg_abc"] = nc.dram_tensor("dbg_abc", [128, 2 * TS], F32, kind="ExternalOutput").ap()
        g["dbg_rows"] = nc.dram_tensor("dbg_rows", [1, 4 * TS], F32, kind="ExternalOutput").ap()

    g["wbf16"] = nc.dram_tensor("wbf16", [128, NTILES * 128], BF16).ap()
    g["xres1"] = nc.dram_tensor("xres1", [D, TLOC], F32).ap()
    g["xres2"] = nc.dram_tensor("xres2", [D, TLOC], F32).ap()
    g["k_own"] = nc.dram_tensor("k_own", [D, TLOC], BF16)
    g["v_own"] = nc.dram_tensor("v_own", [TLOC, D], BF16)
    g["k_full"] = nc.dram_tensor("k_full", [2 * D, TLOC], BF16)
    g["v_full"] = nc.dram_tensor("v_full", [2 * TLOC, D], BF16)

    e = ctx.enter_context
    g["wring"] = e(nc.sbuf_tensor("wring", [128, RING * 128], BF16))[:]
    g["gw_sb"] = e(nc.sbuf_tensor("gw_sb", [128, 128], F32))[:]
    g["E8"] = e(nc.sbuf_tensor("E8", [8, 1024], F32))[:]
    g["I8"] = e(nc.sbuf_tensor("I8", [8, 8], F32))[:]
    g["I128f"] = e(nc.sbuf_tensor("I128f", [128, 128], F32))[:]
    g["I128b"] = e(nc.sbuf_tensor("I128b", [128, 128], BF16))[:]
    g["onesb"] = e(nc.sbuf_tensor("onesb", [128, 8], BF16))[:]
    g["onesrow"] = e(nc.sbuf_tensor("onesrow", [1, 128], F32))[:]
    g["onescol"] = e(nc.sbuf_tensor("onescol", [128, 1], F32))[:]
    g["onescol8"] = e(nc.sbuf_tensor("onescol8", [128, 8], F32))[:]
    g["colsumT"] = e(nc.sbuf_tensor("colsumT", [1, 8], F32))[:]
    g["mp0"] = e(nc.sbuf_tensor("mp0", [128, 128], BF16))[:]
    g["mp1"] = e(nc.sbuf_tensor("mp1", [128, 128], BF16))[:]
    g["mneg"] = e(nc.sbuf_tensor("mneg", [128, 128], BF16))[:]

    for (dst, src, q) in [
        (g["gw_sb"], g["gwpm"], "sync"),
        (g["E8"], g["consts"][0:8, 0:1024], "sync"),
        (g["I8"], g["consts"][0:8, 1024:1032], "sync"),
        (g["I128f"], g["consts"][0:128, 1152:1280], "sync"),
        (g["I128b"], g["consts"][0:128, 1152:1280], "gpsimd"),
        (g["onesb"], g["consts"][0:128, 1432:1440], "gpsimd"),
        (g["onescol"], g["consts"][0:128, 1440:1441], "sync"),
        (g["onescol8"], g["consts"][0:128, 1424:1432], "sync"),
        (g["onesrow"], g["consts"][0:1, 1281:1409], "sync"),
        (g["mp0"], g["mask0"], "gpsimd"),
        (g["mp1"], g["mask1"], "gpsimd"),
        (g["mneg"], g["consts"][0:128, 1536:1664], "gpsimd"),
    ]:
        mk.dma("x" if q == "sync" else "wg",
               lambda e2, d=dst, s=src: e2.dma_start(out=d, in_=s, allow_slow_non_contiguous=True))
    const_w = _merge(mk.pool_now("x"), mk.pool_now("wg"))

    bar = list(const_w)
    if "ffn1" in phases:
        dst = g["xres1"] if "attn" in phases else g["out"]
        _emit_ffn(mk, g["x_in"], dst, bar, colsum=True, tag="f1")
        bar = _bar(mk)
    if "attn" in phases:
        src = g["xres1"] if "ffn1" in phases else g["x_in"]
        dst = g["xres2"] if "ffn3" in phases else g["out"]
        _emit_attn(mk, src, dst, _merge(bar, const_w))
        bar = _bar(mk)
    if "ffn3" in phases:
        src = g["xres2"] if "attn" in phases else g["x_in"]
        _emit_ffn(mk, src, g["out"], bar, colsum="ffn1" not in phases, tag="f3")

    mk.step("sync", lambda e2: e2.nop(), waits=mk.pool_now("out"))

    block = ctx.enter_context(nc.Block())
    for name in ("sync", "gpsimd", "tensor", "scalar", "vector"):
        def emitter(eng, name=name):
            for (e2, fn, waits, incs) in mk.plan:
                if e2 != name:
                    continue
                for (sem, v) in waits:
                    eng.wait_ge(sem.h, v)
                inst = fn(eng)
                for (sem, amt) in incs:
                    inst.then_inc(sem.h, amt)
        getattr(block, name)(emitter)
    ctx.close()
    return nc


def _bar(mk):
    return _merge([(mk.s_pe, mk.s_pe.n), (mk.s_act, mk.s_act.n), (mk.s_dve, mk.s_dve.n)],
                  mk.pool_now("out"))



def _ln_rows(mk, rows, prev_act):
    s_act, s_dve = mk.s_act, mk.s_dve
    w0 = [prev_act] if prev_act else []
    mk.step("vector", lambda e: e.tensor_scalar(rows["negm"], rows["s1r"], -1.0 / D, None, ALU.mult), waits=w0)
    mk.step("vector", lambda e: e.tensor_scalar(rows["rt1"], rows["s2r"], 1.0 / D, 1e-5, ALU.mult, ALU.add))
    mk.step("vector", lambda e: e.tensor_tensor(rows["rt2"], rows["negm"], rows["negm"], ALU.mult))
    mk.step("vector", lambda e: e.tensor_tensor(rows["rt1"], rows["rt1"], rows["rt2"], ALU.subtract),
            incs=[(s_dve, 1)]); s_dve.inc()
    mk.step("scalar", lambda e: e.activation(rows["rt2"], rows["rt1"], AF.Ln),
            waits=[(s_dve, s_dve.n)])
    mk.step("scalar", lambda e: e.activation(rows["rstd"], rows["rt2"], AF.Exp, scale=-0.5),
            incs=[(s_act, 1)]); s_act.inc()
    rstd_act = (s_act, s_act.n)
    mk.step("vector", lambda e: e.tensor_tensor(rows["brow"], rows["negm"], rows["rstd"], ALU.mult),
            waits=[rstd_act], incs=[(s_dve, 1)]); s_dve.inc()
    return (s_dve, s_dve.n), rstd_act


def _bcast_rows(mk, b0, b1, asrc, bsrc, abc, rowm):
    g = mk.g
    s_pe, s_dve = mk.s_pe, mk.s_dve
    w = b0.acquire()
    mk.step("tensor", lambda e, a=asrc: e.matmul(b0.ap, g["onesrow"], a, start=True, stop=True),
            waits=_merge([rowm], w), incs=[(s_pe, 1)]); s_pe.inc()
    mk.step("vector", lambda e: e.tensor_copy(abc[:, 0:TS], b0.ap), waits=[(s_pe, s_pe.n)],
            incs=[(s_dve, 1)]); s_dve.inc()
    b0.release(s_dve, s_dve.n)
    w = b1.acquire()
    mk.step("tensor", lambda e, b=bsrc: e.matmul(b1.ap, g["onesrow"], b, start=True, stop=True),
            waits=_merge([rowm], w), incs=[(s_pe, 1)]); s_pe.inc()
    mk.step("vector", lambda e: e.tensor_copy(abc[:, TS:2 * TS], b1.ap), waits=[(s_pe, s_pe.n)],
            incs=[(s_dve, 1)]); s_dve.inc()
    b1.release(s_dve, s_dve.n)
    return (s_dve, s_dve.n)



def _emit_ffn(mk, src, dst, bar, colsum, tag="f1"):
    nc, g = mk.nc, mk.g
    s_pe, s_act, s_dve = mk.s_pe, mk.s_act, mk.s_dve
    B = mk.banks
    with ExitStack() as ph:
        e = ph.enter_context
        xring = e(nc.sbuf_tensor(tag + "f_xring", [128, 4 * TS], F32))[:]
        xbr = e(nc.sbuf_tensor(tag + "f_xbr", [128, 2 * TS], BF16))[:]
        hsl = e(nc.sbuf_tensor(tag + "f_h", [128, KC * TS], BF16))[:]
        sqr = e(nc.sbuf_tensor(tag + "f_sq", [128, TS], F32))[:]
        tsilu = e(nc.sbuf_tensor(tag + "f_ts", [128, 2 * TS], BF16))[:]
        ttf = e(nc.sbuf_tensor(tag + "f_ttf", [128, 2 * TS], F32))[:]
        aall = e(nc.sbuf_tensor(tag + "f_aall", [128, 88 * TS], BF16))[:]
        ash = e(nc.sbuf_tensor(tag + "f_ash", [128, FSCH * TS], BF16))[:]
        wbc = e(nc.sbuf_tensor(tag + "f_wbc", [128, NE * TS], BF16))[:]
        abc = e(nc.sbuf_tensor(tag + "f_abc", [128, 2 * TS], F32))[:]
        l8 = e(nc.sbuf_tensor(tag + "f_l8", [8, TS], F32))[:]
        ltok = e(nc.sbuf_tensor(tag + "f_ltok", [128, 32], F32))[:]
        rstdT = e(nc.sbuf_tensor(tag + "f_rstdT", [128, 4], F32))[:]
        rt = e(nc.sbuf_tensor(tag + "f_rt", [128, 32], F32))[:]
        w8 = e(nc.sbuf_tensor(tag + "f_w8", [8, TS], F32))[:]
        xout = e(nc.sbuf_tensor(tag + "f_xout", [128, 2 * TS], F32))[:]
        xdr = e(nc.sbuf_tensor(tag + "f_xdr", [128, 2 * TS], F32))[:]
        rows = {nm: e(nc.sbuf_tensor(tag + "f_" + nm, [1, TS], F32))[:]
                for nm in ("s1r", "s2r", "rt1", "rt2", "negm", "rstd")}
        rows["brow"] = rows["rt2"]

        bM = [B[0], B[1], B[2]]
        bG = [B[3], B[4]]
        bU = [B[5], B[6]]

        if colsum:
            w = _merge(bM[0].acquire(), bar)
            gw = g["gw_sb"]
            for k in range(KC):
                mk.step("tensor", lambda e2, l=gw[:, 8 * k:8 * k + 8], st=(k == 0), sp=(k == KC - 1):
                        e2.matmul(bM[0].ap[0:8, 0:1], l, g["onescol"], start=st, stop=sp),
                        waits=w if k == 0 else ())
            cs_pe = mk.t_inc()
            mk.step("vector", lambda e2: e2.tensor_copy(l8[0:8, 0:1], bM[0].ap[0:8, 0:1]),
                    waits=[cs_pe], incs=[(s_dve, 1)]); s_dve.inc()
            bM[0].release(s_dve, s_dve.n)
            w = _merge(bM[1].acquire(), [(s_dve, s_dve.n)])
            mk.step("tensor", lambda e2: e2.matmul(bM[1].ap[0:1, 0:8], l8[0:8, 0:1], g["I8"], start=True, stop=True),
                    waits=w, incs=[(s_pe, 1)]); s_pe.inc()
            mk.step("vector", lambda e2: e2.tensor_copy(g["colsumT"], bM[1].ap[0:1, 0:8]),
                    waits=[(s_pe, s_pe.n)], incs=[(s_dve, 1)]); s_dve.inc()
            bM[1].release(s_dve, s_dve.n)

        free_xring = [list(bar) for _ in range(4)]
        free_xb = [list(bar), list(bar)]
        free_sq = [[], []]
        free_ts = [[], []]
        free_tt = [[], []]
        free_h = list(bar)
        free_rows = list(bar)
        free_aw = list(bar)
        free_xdr = [list(bar), list(bar)]
        free_xout = [[], []]
        prev_rstd_act = None
        prev_exp_act = None

        for s in range(NSL):
            lgB = bM[2]
            lgw = lgB.acquire()
            s1B, s2B = bM[0], bM[1]
            s1w, s2w = s1B.acquire(), s2B.acquire()
            for k in range(KC):
                xs = xring[:, (k % 4) * TS:(k % 4 + 1) * TS]
                xw = mk.dma("x", lambda e2, d=xs, sp=src[128 * k:128 * (k + 1), TS * s:TS * (s + 1)]:
                            e2.dma_start(out=d, in_=sp),
                            waits=_merge(free_xring[k % 4]))
                sq = sqr[:, 0:TS]
                mk.step("vector", lambda e2, d=sq, a=xs: e2.tensor_tensor(d, a, a, ALU.mult),
                        waits=_merge([xw], free_sq[0]), incs=[(s_dve, 1)]); s_dve.inc()
                sq_done = (s_dve, s_dve.n)
                mk.step("tensor", lambda e2, a=xs, st=(k == 0), sp=(k == KC - 1):
                        e2.matmul(s1B.ap[0:8, :], g["onescol8"], a, start=st, stop=sp),
                        waits=_merge([xw], s1w if k == 0 else []))
                mk.step("tensor", lambda e2, a=sq, st=(k == 0), sp=(k == KC - 1):
                        e2.matmul(s2B.ap[0:8, :], g["onescol8"], a, start=st, stop=sp),
                        waits=_merge([sq_done], s2w if k == 0 else []))
                mk.step("tensor", lambda e2, l=g["gw_sb"][:, 8 * k:8 * k + 8], a=xs, st=(k == 0):
                        e2.matmul(lgB.ap[0:8, :], l, a, start=st, stop=False),
                        waits=_merge([xw], lgw if k == 0 else []), incs=[(s_pe, 1)]); s_pe.inc()
                free_xring[k % 4] = [(s_pe, s_pe.n)]
                free_sq[0] = [(s_pe, s_pe.n)]
            stats_pe = (s_pe, s_pe.n)

            mk.step("vector", lambda e2: e2.tensor_copy(rows["s1r"], s1B.ap[0:1, :]),
                    waits=_merge([stats_pe], free_rows), incs=[(s_dve, 1)]); s_dve.inc()
            mk.step("vector", lambda e2: e2.tensor_copy(rows["s2r"], s2B.ap[0:1, :]),
                    incs=[(s_dve, 1)]); s_dve.inc()
            s1B.release(s_dve, s_dve.n)
            s2B.release(s_dve, s_dve.n)
            rowm, rstd_act = _ln_rows(mk, rows, prev_rstd_act)
            prev_rstd_act = rstd_act
            mk.step("tensor", lambda e2: e2.matmul(lgB.ap[0:8, :], g["colsumT"], rows["negm"], start=False, stop=True),
                    waits=[rowm], incs=[(s_pe, 1)]); s_pe.inc()
            mk.step("vector", lambda e2: e2.tensor_copy(l8, lgB.ap[0:8, :]),
                    waits=[(s_pe, s_pe.n)], incs=[(s_dve, 1)]); s_dve.inc()
            lgB.release(s_dve, s_dve.n)
            ab_done = _bcast_rows(mk, bM[0], bM[1], rows["rstd"], rows["brow"], abc, rowm)

            for tc in range(4):
                tb = bM[2] if tc % 2 == 0 else bM[0]
                tw = tb.acquire()
                mk.step("tensor", lambda e2, l=l8[:, 128 * tc:128 * (tc + 1)], ap=tb.ap:
                        e2.matmul(ap[:, 0:8], l, g["I8"], start=True, stop=True),
                        waits=_merge(tw, [(s_dve, s_dve.n)]))
                mk.step("tensor", lambda e2, r=rows["rstd"][0:1, 128 * tc:128 * (tc + 1)], ap=tb.ap:
                        e2.matmul(ap[:, 8:9], r, g["onesrow"][0:1, 0:1], start=True, stop=True),
                        incs=[(s_pe, 1)]); s_pe.inc()
                mk.step("vector", lambda e2, d=ltok[:, 8 * tc:8 * (tc + 1)], ap=tb.ap: e2.tensor_copy(d, ap[:, 0:8]),
                        waits=[(s_pe, s_pe.n)])
                mk.step("vector", lambda e2, d=rstdT[:, tc:tc + 1], ap=tb.ap: e2.tensor_copy(d, ap[:, 8:9]),
                        incs=[(s_dve, 1)]); s_dve.inc()
                tb.release(s_dve, s_dve.n)
            wtok_done = None
            for tc in range(4):
                lt = ltok[:, 8 * tc:8 * (tc + 1)]
                r1 = rt[:, 0:8]; r2 = rt[:, 8:16]; r3 = rt[:, 16:24]
                m1 = rt[:, 24:25]; m2 = rt[:, 25:26]
                rs = rstdT[:, tc:tc + 1]
                w0 = [prev_exp_act] if prev_exp_act else []

                def vstep(fn, extra=()):
                    mk.step("vector", fn, waits=_merge([(s_dve, s_dve.n)], extra),
                            incs=[(s_dve, 1)])
                    s_dve.inc()

                vstep(lambda e2, lt=lt, rs=rs: e2.tensor_scalar(r1, lt, rs, None, ALU.mult), w0)
                vstep(lambda e2: e2.tensor_reduce(m1, r1, mybir.AxisListType.X, ALU.max))
                vstep(lambda e2: e2.tensor_scalar(r1, r1, m1, None, ALU.subtract))
                mk.step("scalar", lambda e2: e2.activation(r1, r1, AF.Exp),
                        waits=[(s_dve, s_dve.n)], incs=[(s_act, 1)]); s_act.inc()
                prev_exp_act = (s_act, s_act.n)
                vstep(lambda e2: e2.tensor_reduce(m1, r1, mybir.AxisListType.X, ALU.add),
                      [(s_act, s_act.n)])
                vstep(lambda e2: e2.reciprocal(m2, m1))
                vstep(lambda e2: e2.tensor_scalar(r1, r1, m2, None, ALU.mult))
                vstep(lambda e2: e2.tensor_reduce(m1, r1, mybir.AxisListType.X, ALU.max))
                vstep(lambda e2: e2.tensor_scalar(r2, r1, m1, None, ALU.is_ge))
                vstep(lambda e2: e2.scalar_tensor_tensor(r3, r2, -4.0, r1, ALU.mult, ALU.add))
                vstep(lambda e2: e2.tensor_reduce(m2, r3, mybir.AxisListType.X, ALU.max))
                vstep(lambda e2: e2.tensor_scalar(r2, r1, m2, None, ALU.is_ge))
                vstep(lambda e2: e2.tensor_tensor(r1, r1, r2, ALU.mult))
                tb = bM[1] if tc % 2 == 0 else bM[2]
                tw = tb.acquire()
                mk.step("tensor", lambda e2, ap=tb.ap: e2.matmul(ap[0:8, 0:128], r1, g["I128f"], start=True, stop=True),
                        waits=_merge(tw, [(s_dve, s_dve.n)]), incs=[(s_pe, 1)]); s_pe.inc()
                mk.step("vector", lambda e2, d=w8[:, 128 * tc:128 * (tc + 1)], ap=tb.ap: e2.tensor_copy(d, ap[0:8, 0:128]),
                        waits=[(s_pe, s_pe.n)], incs=[(s_dve, 1)]); s_dve.inc()
                tb.release(s_dve, s_dve.n)
                wtok_done = (s_dve, s_dve.n)
            for n in range(NE):
                tb = bM[n % 3]
                tw = tb.acquire()
                mk.step("tensor", lambda e2, l=g["E8"][:, 128 * n:128 * (n + 1)], ap=tb.ap:
                        e2.matmul(ap, l, w8, start=True, stop=True),
                        waits=_merge(tw, [wtok_done]), incs=[(s_pe, 1)]); s_pe.inc()
                mk.step("vector", lambda e2, d=wbc[:, TS * n:TS * (n + 1)], ap=tb.ap: e2.tensor_copy(d, ap),
                        waits=_merge([(s_pe, s_pe.n)], free_aw if n == 0 else []),
                        incs=[(s_dve, 1)]); s_dve.inc()
                tb.release(s_dve, s_dve.n)
            wbc_done = (s_dve, s_dve.n)
            if _DEBUG_TAPS and s == 2 and tag == "f1":
                for (i_, rr) in enumerate(("s1r", "s2r", "rstd", "brow")):
                    mk.dma("out", lambda e2, d=g["dbg_rows"][0:1, i_ * TS:(i_ + 1) * TS], sb=rows[rr]:
                           e2.dma_start(out=d, in_=sb),
                           waits=[(s_dve, s_dve.n)])
                free_rows = _merge([(s_pe, s_pe.n)], mk.pool_now("out"))
            else:
                free_rows = [(s_pe, s_pe.n)]

            h_all = None
            h_done = []
            for k in range(KC):
                xs = xring[:, (k % 4) * TS:(k % 4 + 1) * TS]
                xw = mk.dma("x", lambda e2, d=xs, sp=src[128 * k:128 * (k + 1), TS * s:TS * (s + 1)]:
                            e2.dma_start(out=d, in_=sp),
                            waits=_merge(free_xring[k % 4]))
                xb = xbr[:, (k % 2) * TS:(k % 2 + 1) * TS]
                mk.step("scalar", lambda e2, d=xb, sp=xs: e2.copy(d, sp),
                        waits=_merge([xw], free_xb[k % 2]), incs=[(s_act, 1)]); s_act.inc()
                cast_done = (s_act, s_act.n)
                free_xring[k % 4] = [cast_done]
                tt = ttf[:, (k % 2) * TS:(k % 2 + 1) * TS]
                hk = hsl[:, k * TS:(k + 1) * TS]
                mk.step("vector", lambda e2, tt=tt, xb=xb: e2.tensor_tensor(tt, xb, abc[:, 0:TS], ALU.mult),
                        waits=_merge([ab_done, cast_done], free_tt[k % 2], free_h if k == 0 else []))
                mk.step("vector", lambda e2, hk=hk, tt=tt: e2.tensor_tensor(hk, tt, abc[:, TS:2 * TS], ALU.add),
                        incs=[(s_dve, 1)]); s_dve.inc()
                free_tt[k % 2] = [(s_dve, s_dve.n)]
                free_xb[k % 2] = [(s_dve, s_dve.n)]
                h_all = (s_dve, s_dve.n)
                h_done.append((s_dve, s_dve.n))

            units = [(n, f) for n in range(NE) for f in range(FCH)] + [(-1, fs) for fs in range(FSCH)]
            for ui, (n, f) in enumerate(units):
                shared = n < 0
                if shared:
                    tg = [_sgu_tile(f, 0, h) for h in range(KC)]
                    tu = [_sgu_tile(f, 1, h) for h in range(KC)]
                else:
                    tg = [_gu_tile(n, f, 0, h) for h in range(KC)]
                    tu = [_gu_tile(n, f, 1, h) for h in range(KC)]
                slgu, wg = mk.fetch(tg + tu)
                slg, slu = slgu[:KC], slgu[KC:]
                wu = wg
                bg = bG[ui % 2]; bu = bU[ui % 2]
                bgw = bg.acquire(); buw = bu.acquire()
                for h in range(KC):
                    hw = [h_done[h]] if (ui == 0 and h_done) else ([h_all] if h == 0 else [])
                    mk.step("tensor", lambda e2, sl=slg[h], hh=h, st=(h == 0), sp=(h == KC - 1), ap=bg.ap:
                            e2.matmul(ap, mk.wtile(sl), hsl[:, hh * TS:(hh + 1) * TS], start=st, stop=sp),
                            waits=_merge(wg, bgw, hw) if h == 0 else _merge(hw))
                g_done = mk.t_inc()
                mk.ws_mark(slg, s_pe, s_pe.n)
                for h in range(KC):
                    mk.step("tensor", lambda e2, sl=slu[h], hh=h, st=(h == 0), sp=(h == KC - 1), ap=bu.ap:
                            e2.matmul(ap, mk.wtile(sl), hsl[:, hh * TS:(hh + 1) * TS], start=st, stop=sp),
                            waits=_merge(wu, buw) if h == 0 else ())
                u_done = mk.t_inc()
                mk.ws_mark(slu, s_pe, s_pe.n)
                ts = tsilu[:, (ui % 2) * TS:(ui % 2 + 1) * TS]
                mk.step("scalar", lambda e2, ts=ts, ap=bg.ap: e2.activation(ts, ap, AF.Silu),
                        waits=_merge([g_done], free_ts[ui % 2]), incs=[(s_act, 1)]); s_act.inc()
                silu_done = (s_act, s_act.n)
                bg.release(s_act, s_act.n)
                if shared:
                    dap = ash[:, f * TS:(f + 1) * TS]
                    mk.step("vector", lambda e2, d=dap, ts=ts, ap=bu.ap: e2.tensor_tensor(d, ts, ap, ALU.mult),
                            waits=_merge([u_done, silu_done], free_aw), incs=[(s_dve, 1)]); s_dve.inc()
                else:
                    dap = aall[:, (n * FCH + f) * TS:(n * FCH + f + 1) * TS]
                    wn = wbc[:, TS * n:TS * (n + 1)]
                    mk.step("vector", lambda e2, d=dap, ts=ts, ap=bu.ap: e2.tensor_tensor(d, ts, ap, ALU.mult),
                            waits=_merge([u_done, silu_done, wbc_done], free_aw), incs=[(s_dve, 1)]); s_dve.inc()
                    mk.step("vector", lambda e2, d=dap, wn=wn: e2.tensor_tensor(d, d, wn, ALU.mult),
                            incs=[(s_dve, 1)]); s_dve.inc()
                free_ts[ui % 2] = [(s_dve, s_dve.n)]
                bu.release(s_dve, s_dve.n)
            a_ready = (s_dve, s_dve.n)
            free_h = [(s_pe, s_pe.n)]
            if _DEBUG_TAPS and s == 2 and tag == "f1":
                for (dname, sb) in [("dbg_h", hsl), ("dbg_l8", l8), ("dbg_w8", w8), ("dbg_wbc", wbc),
                                    ("dbg_a", aall[:, 0:11 * TS]), ("dbg_as", ash[:, 0:4 * TS]),
                                    ("dbg_abc", abc)]:
                    mk.dma("out", lambda e2, d=g[dname], sb=sb: e2.dma_start(out=d, in_=sb),
                           waits=[a_ready])

            dunits = list(range(88)) + list(range(88, 110))
            for h in range(KC):
                bd = bG[h % 2]
                bdw = bd.acquire()
                first = True
                done_groups = 0
                for c0 in range(0, 110, 28):
                    cids = dunits[c0:c0 + 28]
                    sl, wv = mk.fetch([_down_tile(h, j) for j in cids])
                    for idx, j in enumerate(cids):
                        rhs = aall[:, j * TS:(j + 1) * TS] if j < 88 else ash[:, (j - 88) * TS:(j - 87) * TS]
                        mk.step("tensor", lambda e2, sl2=sl[idx], r=rhs, st=first and idx == 0,
                                sp=(c0 + 28 >= 110 and idx == len(cids) - 1), ap=bd.ap:
                                e2.matmul(ap, mk.wtile(sl2), r, start=st, stop=sp),
                                waits=_merge(wv, bdw, [a_ready]) if (first and idx == 0) else (_merge(wv) if idx == 0 else ()))
                    first = False
                    gv = mk.t_inc()
                    mk.ws_mark(sl, s_pe, s_pe.n)
                d_done = (s_pe, s_pe.n)
                xd = xdr[:, (h % 2) * TS:(h % 2 + 1) * TS]
                xw = mk.dma("x", lambda e2, d=xd, sp=src[128 * h:128 * (h + 1), TS * s:TS * (s + 1)]:
                            e2.dma_start(out=d, in_=sp),
                            waits=_merge(free_xdr[h % 2]))
                xo = xout[:, (h % 2) * TS:(h % 2 + 1) * TS]
                mk.step("vector", lambda e2, xo=xo, xd=xd, ap=bd.ap:
                        e2.scalar_tensor_tensor(xo, ap, 0.5, xd, ALU.mult, ALU.add),
                        waits=_merge([d_done, xw], free_xout[h % 2]), incs=[(s_dve, 1)]); s_dve.inc()
                bd.release(s_dve, s_dve.n)
                free_xdr[h % 2] = [(s_dve, s_dve.n)]
                ov = mk.dma("out", lambda e2, d=dst[128 * h:128 * (h + 1), TS * s:TS * (s + 1)], xo=xo:
                            e2.dma_start(out=d, in_=xo),
                            waits=[(s_dve, s_dve.n)])
                free_xout[h % 2] = [ov]
            free_aw = [(s_pe, s_pe.n)]



def _emit_attn(mk, src, dst, bar):
    nc, g = mk.nc, mk.g
    s_pe, s_act, s_dve = mk.s_pe, mk.s_act, mk.s_dve
    s_cck, s_ccv = mk.s_cck, mk.s_ccv
    B = mk.banks
    SC = 1.0 / math.sqrt(HD)
    with ExitStack() as ph:
        e = ph.enter_context
        q_all = e(nc.sbuf_tensor("a_q", [128, NH * TLOC], BF16))[:]
        qdone = {}

        with ExitStack() as p2:
            e2_ = p2.enter_context
            xring = e2_(nc.sbuf_tensor("a_xring", [128, 4 * TS], F32))[:]
            xbr = e2_(nc.sbuf_tensor("a_xbr", [128, 2 * TS], BF16))[:]
            hsl = e2_(nc.sbuf_tensor("a_h", [128, KC * TS], BF16))[:]
            sqr = e2_(nc.sbuf_tensor("a_sq", [128, 2 * TS], F32))[:]
            abc = e2_(nc.sbuf_tensor("a_abc", [128, 2 * TS], F32))[:]
            ttf = e2_(nc.sbuf_tensor("a_ttf", [128, 2 * TS], F32))[:]
            ctab = e2_(nc.sbuf_tensor("a_ct", [128, TLOC], BF16))[:]
            stab = e2_(nc.sbuf_tensor("a_st", [128, TLOC], BF16))[:]
            kst = e2_(nc.sbuf_tensor("a_kst", [128, 2 * TS], BF16))[:]
            krot = e2_(nc.sbuf_tensor("a_krot", [128, 2 * TS], BF16))[:]
            rtmp = e2_(nc.sbuf_tensor("a_rtmp", [128, 2 * TS], BF16))[:]
            vst = e2_(nc.sbuf_tensor("a_vst", [128, 2 * 2048], BF16))[:]
            a2 = e2_(nc.sbuf_tensor("a_a2", [1, TLOC], F32))[:]
            b2 = e2_(nc.sbuf_tensor("a_b2", [1, TLOC], F32))[:]
            rows = {nm: e2_(nc.sbuf_tensor("a_" + nm, [1, TS], F32))[:]
                    for nm in ("s1r", "s2r", "rt1", "rt2", "negm", "rstd", "brow")}

            t1_ = mk.dma("wg", lambda e3: e3.dma_start(out=ctab, in_=g["ctab_d"]), waits=bar)
            t2_ = mk.dma("wg", lambda e3: e3.dma_start(out=stab, in_=g["stab_d"]), waits=bar)
            tab_w = _merge([t1_], [t2_])

            st_ = {"free_xring": [list(bar) for _ in range(4)],
                   "free_xb": [list(bar), list(bar)],
                   "free_sq": [[], []],
                   "free_tt": [[], []],
                   "free_h": list(bar),
                   "free_rows": list(bar),
                   "prev_rstd": None}
            free_kst = [[], []]
            free_krot = [[], []]
            free_vst = [[], []]

            def ln_pass(s, with_stats):
                if with_stats:
                    s1B, s2B = B[0], B[1]
                    s1w, s2w = s1B.acquire(), s2B.acquire()
                    for k in range(KC):
                        xs = xring[:, (k % 4) * TS:(k % 4 + 1) * TS]
                        xw = mk.dma("x", lambda e3, d=xs, sp=src[128 * k:128 * (k + 1), TS * s:TS * (s + 1)]:
                                    e3.dma_start(out=d, in_=sp),
                                    waits=_merge(st_["free_xring"][k % 4]))
                        sq = sqr[:, (k % 2) * TS:(k % 2 + 1) * TS]
                        mk.step("vector", lambda e3, d=sq, a=xs: e3.tensor_tensor(d, a, a, ALU.mult),
                                waits=_merge([xw], st_["free_sq"][k % 2]), incs=[(s_dve, 1)]); s_dve.inc()
                        sq_done = (s_dve, s_dve.n)
                        mk.step("tensor", lambda e3, a=xs, stt=(k == 0), spp=(k == KC - 1):
                                e3.matmul(s1B.ap[0:8, :], g["onescol8"], a, start=stt, stop=spp),
                                waits=_merge([xw], s1w if k == 0 else []))
                        mk.step("tensor", lambda e3, a=sq, stt=(k == 0), spp=(k == KC - 1):
                                e3.matmul(s2B.ap[0:8, :], g["onescol8"], a, start=stt, stop=spp),
                                waits=_merge([sq_done], s2w if k == 0 else []), incs=[(s_pe, 1)]); s_pe.inc()
                        st_["free_xring"][k % 4] = [(s_pe, s_pe.n)]
                        st_["free_sq"][k % 2] = [(s_pe, s_pe.n)]
                    stats_pe = (s_pe, s_pe.n)
                    mk.step("vector", lambda e3: e3.tensor_copy(rows["s1r"], s1B.ap[0:1, :]),
                            waits=_merge([stats_pe], st_["free_rows"]), incs=[(s_dve, 1)]); s_dve.inc()
                    mk.step("vector", lambda e3: e3.tensor_copy(rows["s2r"], s2B.ap[0:1, :]),
                            incs=[(s_dve, 1)]); s_dve.inc()
                    s1B.release(s_dve, s_dve.n)
                    s2B.release(s_dve, s_dve.n)
                    rowm, rstd_act = _ln_rows(mk, rows, st_["prev_rstd"])
                    st_["prev_rstd"] = rstd_act
                    mk.step("vector", lambda e3, s=s: e3.tensor_copy(a2[0:1, TS * s:TS * (s + 1)], rows["rstd"]))
                    mk.step("vector", lambda e3, s=s: e3.tensor_copy(b2[0:1, TS * s:TS * (s + 1)], rows["brow"]),
                            incs=[(s_dve, 1)]); s_dve.inc()
                    rowm = (s_dve, s_dve.n)
                    asrc, bsrc = rows["rstd"], rows["brow"]
                    st_["free_rows"] = []
                else:
                    rowm = (s_dve, s_dve.n)
                    asrc = a2[0:1, TS * s:TS * (s + 1)]
                    bsrc = b2[0:1, TS * s:TS * (s + 1)]
                ab_done = _bcast_rows(mk, B[0], B[1], asrc, bsrc, abc, rowm)
                h_all = None
                h_done.clear()
                for k in range(KC):
                    xs = xring[:, (k % 4) * TS:(k % 4 + 1) * TS]
                    xw = mk.dma("x", lambda e3, d=xs, sp=src[128 * k:128 * (k + 1), TS * s:TS * (s + 1)]:
                                e3.dma_start(out=d, in_=sp),
                                waits=_merge(st_["free_xring"][k % 4]))
                    xb = xbr[:, (k % 2) * TS:(k % 2 + 1) * TS]
                    mk.step("scalar", lambda e3, d=xb, sp=xs: e3.copy(d, sp),
                            waits=_merge([xw], st_["free_xb"][k % 2]), incs=[(s_act, 1)]); s_act.inc()
                    cast_done = (s_act, s_act.n)
                    st_["free_xring"][k % 4] = [cast_done]
                    tt = ttf[:, (k % 2) * TS:(k % 2 + 1) * TS]
                    hk = hsl[:, k * TS:(k + 1) * TS]
                    mk.step("vector", lambda e3, tt=tt, xb=xb: e3.tensor_tensor(tt, xb, abc[:, 0:TS], ALU.mult),
                            waits=_merge([ab_done, cast_done], st_["free_tt"][k % 2],
                                         st_["free_h"] if k == 0 else []))
                    mk.step("vector", lambda e3, hk=hk, tt=tt: e3.tensor_tensor(hk, tt, abc[:, TS:2 * TS], ALU.add),
                            incs=[(s_dve, 1)]); s_dve.inc()
                    st_["free_tt"][k % 2] = [(s_dve, s_dve.n)]
                    st_["free_xb"][k % 2] = [(s_dve, s_dve.n)]
                    h_all = (s_dve, s_dve.n)
                    h_done.append(h_all)
                return h_all

            def rope(src_ap, dst_ap, cs, ss, extra_w):
                t1 = rtmp[0:64, 0:TS]; t2 = rtmp[0:64, TS:2 * TS]
                t3 = rtmp[64:128, 0:TS]; t4 = rtmp[64:128, TS:2 * TS]
                mk.step("vector", lambda e3: e3.tensor_tensor(t1, src_ap[0:64, :], cs[0:64, :], ALU.mult),
                        waits=_merge(extra_w, tab_w))
                mk.step("vector", lambda e3: e3.tensor_tensor(t2, src_ap[64:128, :], ss[64:128, :], ALU.mult))
                mk.step("vector", lambda e3: e3.tensor_tensor(t3, src_ap[64:128, :], cs[64:128, :], ALU.mult))
                mk.step("vector", lambda e3: e3.tensor_tensor(t4, src_ap[0:64, :], ss[0:64, :], ALU.mult))
                mk.step("vector", lambda e3: e3.tensor_tensor(dst_ap[0:64, :], t1, t2, ALU.subtract))
                mk.step("vector", lambda e3: e3.tensor_tensor(dst_ap[64:128, :], t3, t4, ALU.add),
                        incs=[(s_dve, 1)]); s_dve.inc()
                return (s_dve, s_dve.n)

            bQ = [B[2], B[3]]
            h_done = []
            k_store_w = []
            for s in range(NSL):
                h_all = ln_pass(s, with_stats=True)
                cs = ctab[0:128, TS * s:TS * (s + 1)]
                ss = stab[0:128, TS * s:TS * (s + 1)]
                for oc in range(NH):
                    sl, wv = mk.fetch([_qk_tile(oc, h) for h in range(KC)])
                    bq = bQ[oc % 2]
                    bqw = bq.acquire()
                    for h in range(KC):
                        hw = [h_done[h]] if (oc == 0 and h_done) else ([h_all] if h == 0 else [])
                        mk.step("tensor", lambda e3, sl2=sl[h], hh=h, stt=(h == 0), spp=(h == KC - 1), ap=bq.ap:
                                e3.matmul(ap, mk.wtile(sl2), hsl[:, hh * TS:(hh + 1) * TS], start=stt, stop=spp),
                                waits=_merge(wv, bqw, hw) if h == 0 else _merge(hw))
                    kd = mk.t_inc()
                    mk.ws_mark(sl, s_pe, s_pe.n)
                    kb = kst[:, (oc % 2) * TS:(oc % 2 + 1) * TS]
                    mk.step("scalar", lambda e3, d=kb, ap=bq.ap: e3.copy(d, ap),
                            waits=_merge([kd], free_kst[oc % 2]), incs=[(s_act, 1)]); s_act.inc()
                    bq.release(s_act, s_act.n)
                    kr = krot[:, (oc % 2) * TS:(oc % 2 + 1) * TS]
                    rd = rope(kb, kr, cs, ss, _merge([(s_act, s_act.n)], free_krot[oc % 2]))
                    free_kst[oc % 2] = [rd]
                    kv_ = mk.dma("kv", lambda e3, d=g["k_own"][:][128 * oc:128 * (oc + 1), TS * s:TS * (s + 1)], kr=kr:
                                 e3.dma_start(out=d, in_=kr),
                                 waits=[rd])
                    k_store_w.append(kv_)
                    free_krot[oc % 2] = [kv_]
                for p in range(2):
                    rA = mk.region(4, 8)
                    rB = mk.region(0, 4)
                    rAw = rA.acquire(); rBw = rB.acquire()
                    for d in range(KC):
                        sl, wv = mk.fetch([_v_tile(d, vc) for vc in range(16)], align16=True)
                        for tt_i in range(2):
                            ti = 2 * p + tt_i
                            lhsT = hsl[:, d * TS + 128 * ti:d * TS + 128 * (ti + 1)]
                            reg = rA if tt_i == 0 else rB
                            for vs in range(4):
                                r0 = sl[4 * vs]
                                rhs = mk.wtile(r0, 4)
                                extra = _merge(wv, rAw, rBw, [h_all]) if (d == 0 and tt_i == 0 and vs == 0) \
                                    else (_merge(wv) if (tt_i == 0 and vs == 0) else ())
                                mk.step("tensor", lambda e3, l=lhsT, r=rhs, stt=(d == 0), spp=(d == KC - 1),
                                        ap=reg.ap, vs=vs:
                                        e3.matmul(ap[:, 512 * vs:512 * (vs + 1)], l, r, start=stt, stop=spp),
                                        waits=extra)
                        gv = mk.t_inc()
                        mk.ws_mark(sl, s_pe, s_pe.n)
                    v_done = (s_pe, s_pe.n)
                    for tt_i in range(2):
                        reg = rA if tt_i == 0 else rB
                        vsb = vst[:, tt_i * 2048:(tt_i + 1) * 2048]
                        mk.step("vector", lambda e3, d=vsb, ap=reg.ap: e3.tensor_copy(d, ap),
                                waits=_merge([v_done], free_vst[tt_i]), incs=[(s_dve, 1)]); s_dve.inc()
                        reg.release(s_dve, s_dve.n)
                        ti = 4 * s + 2 * p + tt_i
                        kv_ = mk.dma("kv", lambda e3, d=g["v_own"][:][128 * ti:128 * (ti + 1), :], vsb=vsb:
                                     e3.dma_start(out=d, in_=vsb),
                                     waits=[(s_dve, s_dve.n)])
                        free_vst[tt_i] = [kv_]
                st_["free_h"] = [(s_pe, s_pe.n)]

            mk.step("gpsimd", lambda e3: e3.collective_compute(
                "AllGather", ALU.bypass, replica_groups=[[0, 1], [2, 3], [4, 5], [6, 7]],
                ins=[g["k_own"].ap().opt()], outs=[g["k_full"].ap().opt()]),
                waits=_merge(*[[w] for w in k_store_w]), incs=[(s_cck, 1)]); s_cck.inc()
            mk.step("gpsimd", lambda e3: e3.collective_compute(
                "AllGather", ALU.bypass, replica_groups=[[0, 1], [2, 3], [4, 5], [6, 7]],
                ins=[g["v_own"].ap().opt()], outs=[g["v_full"].ap().opt()]),
                waits=mk.pool_now("kv"), incs=[(s_ccv, 1)]); s_ccv.inc()

            for s in range(NSL):
                h_all = ln_pass(s, with_stats=False)
                cs = ctab[0:128, TS * s:TS * (s + 1)]
                ss = stab[0:128, TS * s:TS * (s + 1)]
                for oc in range(NH):
                    sl, wv = mk.fetch([_qk_tile(16 + oc, h) for h in range(KC)])
                    bq = bQ[oc % 2]
                    bqw = bq.acquire()
                    for h in range(KC):
                        hw = [h_done[h]] if (oc == 0 and h_done) else ([h_all] if h == 0 else [])
                        mk.step("tensor", lambda e3, sl2=sl[h], hh=h, stt=(h == 0), spp=(h == KC - 1), ap=bq.ap:
                                e3.matmul(ap, mk.wtile(sl2), hsl[:, hh * TS:(hh + 1) * TS], start=stt, stop=spp),
                                waits=_merge(wv, bqw, hw) if h == 0 else _merge(hw))
                    kd = mk.t_inc()
                    mk.ws_mark(sl, s_pe, s_pe.n)
                    kb = kst[:, (oc % 2) * TS:(oc % 2 + 1) * TS]
                    mk.step("scalar", lambda e3, d=kb, ap=bq.ap: e3.copy(d, ap),
                            waits=_merge([kd], free_kst[oc % 2]), incs=[(s_act, 1)]); s_act.inc()
                    bq.release(s_act, s_act.n)
                    qd = q_all[:, oc * TLOC + TS * s:oc * TLOC + TS * (s + 1)]
                    rd = rope(kb, qd, cs, ss, [(s_act, s_act.n)])
                    free_kst[oc % 2] = [rd]
                    qdone[(oc, s)] = rd
                st_["free_h"] = [(s_pe, s_pe.n)]
            p2a_done = _bar(mk)

        with ExitStack() as p3:
            e3_ = p3.enter_context
            attn = e3_(nc.sbuf_tensor("a_attn", [128, NH * TLOC], BF16))[:]
            attn_all = None
            with ExitStack() as pf:
                e4_ = pf.enter_context
                kh = e4_(nc.sbuf_tensor("a_kh", [128, 2 * 4096], BF16))[:]
                vt = e4_(nc.sbuf_tensor("a_vt", [128, 2 * 4096], BF16))[:]
                es = e4_(nc.sbuf_tensor("a_es", [128, 3 * 1024], BF16))[:]
                rbc = e4_(nc.sbuf_tensor("a_rbc", [128, 2 * TS], F32))[:]
                rcp = e4_(nc.sbuf_tensor("a_rcp", [1, 2 * TS], F32))[:]
                kf = g["k_full"].ap()
                vf = g["v_full"].ap()
                bST = [mk.region(0, 2), mk.region(2, 4)]
                bAV = [B[4], B[5]]
                bDN = [B[6], B[7]]
                free_kh = [list(p2a_done), list(p2a_done)]
                free_vt = [list(p2a_done), list(p2a_done)]
                free_es = [list(p2a_done) for _ in range(3)]
                free_rbc = [list(p2a_done), list(p2a_done)]
                khl = {}

                def load_head(hh):
                    buf = hh % 2
                    w = _merge(free_kh[buf], [(s_cck, 1)])
                    ws_ = []
                    for r in range(2):
                        dd = kh[:, buf * 4096 + r * 2048:buf * 4096 + (r + 1) * 2048]
                        sp = kf[r * D + 128 * hh:r * D + 128 * (hh + 1), :]
                        ws_.append(mk.dma("khl", lambda e5, d=dd, s2=sp: e5.dma_start(out=d, in_=s2),
                                          waits=w))
                    w2 = _merge(free_vt[buf], [(s_ccv, 1)])
                    for r in range(2):
                        dd = vt[:, buf * 4096 + r * 2048:buf * 4096 + (r + 1) * 2048].rearrange(
                            "p (m c) -> p m c", c=128)
                        sp = vf[r * TLOC:(r + 1) * TLOC, 128 * hh:128 * (hh + 1)].rearrange(
                            "(m p) c -> p m c", p=128)
                        ws_.append(mk.dma("khl", lambda e5, d=dd, s2=sp: e5.dma_start(out=d, in_=s2),
                                          waits=w2))
                    khl[hh] = _merge(*[[x] for x in ws_])

                load_head(0)
                uu = 0
                for hh in range(NH):
                    if hh + 1 < NH:
                        load_head(hh + 1)
                    buf = hh % 2
                    for s in range(NSL):
                        n_k = 8 * s + 8
                        bav = bAV[uu % 2]; bdn = bDN[uu % 2]
                        avw = bav.acquire(); dnw = bdn.acquire()
                        qw = qdone[(hh, s)]
                        rhs_q = q_all[:, hh * TLOC + TS * s:hh * TLOC + TS * (s + 1)]
                        pend_av = [None]

                        def emit_scores(jp):
                            bst = bST[jp % 2]
                            stw = bst.acquire()
                            for jj, j in enumerate((2 * jp, 2 * jp + 1)):
                                r, m = j % 2, j // 2
                                lhsT = kh[:, buf * 4096 + r * 2048 + 128 * m:buf * 4096 + r * 2048 + 128 * (m + 1)]
                                e_ = j - 8 * s
                                masks = []
                                if 0 <= e_ // 2 < 4 and e_ >= 0:
                                    masks.append((e_ // 2, g["mp0"] if e_ % 2 == 0 else g["mp1"]))
                                for qq in range(4):
                                    if e_ > 2 * qq + 1:
                                        masks.append((qq, g["mneg"]))
                                mk.step("tensor", lambda e5, l=lhsT, jj=jj, ap=bst.ap, nm=bool(masks), r2=rhs_q:
                                        e5.matmul(ap[:, 512 * jj:512 * (jj + 1)], l, r2, start=True, stop=not nm),
                                        waits=_merge(stw, khl[hh], [qw]) if jj == 0 else ())
                                for mi, (qi, msk) in enumerate(masks):
                                    mk.step("tensor", lambda e5, m2=msk, jj=jj, qi=qi, ap=bst.ap,
                                            sp=(mi == len(masks) - 1):
                                            e5.matmul(ap[:, 512 * jj + 128 * qi:512 * jj + 128 * (qi + 1)],
                                                      g["I128b"], m2, start=False, stop=sp))
                            sdone = mk.t_inc()
                            esb = es[:, (jp % 3) * 1024:(jp % 3 + 1) * 1024]
                            mk.step("scalar", lambda e5, d=esb, ap=bst.ap: e5.activation(d, ap, AF.Exp, scale=SC),
                                    waits=_merge([sdone], free_es[jp % 3]), incs=[(s_act, 1)]); s_act.inc()
                            bst.release(s_act, s_act.n)
                            return (jp, esb, (s_act, s_act.n))

                        def emit_av(state):
                            jp, esb, e_done = state
                            for jj, j in enumerate((2 * jp, 2 * jp + 1)):
                                r, m = j % 2, j // 2
                                esl = esb[:, 512 * jj:512 * (jj + 1)]
                                mk.step("tensor", lambda e5, esl=esl, stt=(j == 0), spp=(j == n_k - 1), ap=bdn.ap:
                                        e5.matmul(ap[0:8, :], g["onesb"], esl, start=stt, stop=spp),
                                        waits=_merge([e_done], dnw if j == 0 else []) if jj == 0 else ())
                                lhsT = vt[:, buf * 4096 + r * 2048 + 128 * m:buf * 4096 + r * 2048 + 128 * (m + 1)]
                                mk.step("tensor", lambda e5, l=lhsT, esl=esl, stt=(j == 0), spp=(j == n_k - 1), ap=bav.ap:
                                        e5.matmul(ap, l, esl, start=stt, stop=spp),
                                        waits=_merge(avw, khl[hh]) if j == 0 else ())
                            fe = mk.t_inc()
                            free_es[jp % 3] = [fe]

                        for jp in range(n_k // 2):
                            st8 = emit_scores(jp)
                            if pend_av[0] is not None:
                                emit_av(pend_av[0])
                            pend_av[0] = st8
                        emit_av(pend_av[0])
                        pend_av[0] = None
                        unit_pe = (s_pe, s_pe.n)
                        rc = rcp[0:1, (uu % 2) * TS:(uu % 2 + 1) * TS]
                        mk.step("vector", lambda e5, rc=rc, ap=bdn.ap: e5.reciprocal(rc, ap[0:1, :]),
                                waits=_merge([unit_pe], free_rbc[uu % 2]), incs=[(s_dve, 1)]); s_dve.inc()
                        bdn.release(s_dve, s_dve.n)
                        bb = bST[0]
                        bbw = bb.acquire()
                        mk.step("tensor", lambda e5, rc=rc, ap=bb.ap:
                                e5.matmul(ap[:, 0:TS], g["onesrow"], rc, start=True, stop=True),
                                waits=_merge(bbw, [(s_dve, s_dve.n)]), incs=[(s_pe, 1)]); s_pe.inc()
                        rb = rbc[:, (uu % 2) * TS:(uu % 2 + 1) * TS]
                        mk.step("vector", lambda e5, rb=rb, ap=bb.ap: e5.tensor_copy(rb, ap[:, 0:TS]),
                                waits=[(s_pe, s_pe.n)], incs=[(s_dve, 1)]); s_dve.inc()
                        bb.release(s_dve, s_dve.n)
                        dd = attn[:, hh * TLOC + TS * s:hh * TLOC + TS * (s + 1)]
                        mk.step("vector", lambda e5, d=dd, rb=rb, ap=bav.ap:
                                e5.scalar_tensor_tensor(d, ap, 1.0, rb, ALU.mult, ALU.mult),
                                incs=[(s_dve, 1)]); s_dve.inc()
                        bav.release(s_dve, s_dve.n)
                        free_rbc[uu % 2] = [(s_dve, s_dve.n)]
                        uu += 1
                    free_kh[buf] = [(s_pe, s_pe.n)]
                    free_vt[buf] = [(s_pe, s_pe.n)]
                attn_all = (s_dve, s_dve.n)
                fl_done = _bar(mk)

            with ExitStack() as p4:
                e5_ = p4.enter_context
                x1r = e5_(nc.sbuf_tensor("a_x1r", [128, 2 * TS], F32))[:]
                x2o = e5_(nc.sbuf_tensor("a_x2o", [128, 2 * TS], F32))[:]
                bO = [B[0], B[1]]
                free_x1r = [list(fl_done), list(fl_done)]
                free_x2o = [[], []]
                for s in range(NSL):
                    for oc in range(KC):
                        sl, wv = mk.fetch([_wo_tile(oc, k) for k in range(KC)])
                        bo = bO[oc % 2]
                        bow = bo.acquire()
                        for k in range(KC):
                            rhs = attn[:, k * TLOC + TS * s:k * TLOC + TS * (s + 1)]
                            mk.step("tensor", lambda e5, sl2=sl[k], r=rhs, stt=(k == 0), spp=(k == KC - 1), ap=bo.ap:
                                    e5.matmul(ap, mk.wtile(sl2), r, start=stt, stop=spp),
                                    waits=_merge(wv, bow, [attn_all]) if k == 0 else ())
                        od = mk.t_inc()
                        mk.ws_mark(sl, s_pe, s_pe.n)
                        xd = x1r[:, (oc % 2) * TS:(oc % 2 + 1) * TS]
                        xw = mk.dma("x", lambda e5, d=xd, sp=src[128 * oc:128 * (oc + 1), TS * s:TS * (s + 1)]:
                                    e5.dma_start(out=d, in_=sp),
                                    waits=_merge(free_x1r[oc % 2]))
                        xo = x2o[:, (oc % 2) * TS:(oc % 2 + 1) * TS]
                        mk.step("vector", lambda e5, xo=xo, xd=xd, ap=bo.ap:
                                e5.scalar_tensor_tensor(xo, ap, 1.0, xd, ALU.mult, ALU.add),
                                waits=_merge([od, xw], free_x2o[oc % 2]), incs=[(s_dve, 1)]); s_dve.inc()
                        bo.release(s_dve, s_dve.n)
                        free_x1r[oc % 2] = [(s_dve, s_dve.n)]
                        ov = mk.dma("out", lambda e5, d=dst[128 * oc:128 * (oc + 1), TS * s:TS * (s + 1)], xo=xo:
                                    e5.dma_start(out=d, in_=xo),
                                    waits=[(s_dve, s_dve.n)])
                        free_x2o[oc % 2] = [ov]



def _build_wblob(w_qkv, w_o, gate_w, up_w, down_w, sh_g, sh_u, sh_d):
    tiles = np.empty((NTILES, 128, 128), dtype=np.float32)
    G = gate_w.transpose(0, 2, 1).reshape(NE, KC, 128, FCH, 128)
    U = up_w.transpose(0, 2, 1).reshape(NE, KC, 128, FCH, 128)
    gu = np.stack([G.transpose(0, 3, 1, 2, 4), U.transpose(0, 3, 1, 2, 4)], axis=2)
    tiles[0:N_GU] = gu.reshape(N_GU, 128, 128)
    SG = sh_g.T.reshape(KC, 128, FSCH, 128).transpose(2, 0, 1, 3)
    SU = sh_u.T.reshape(KC, 128, FSCH, 128).transpose(2, 0, 1, 3)
    tiles[N_GU:N_GU + N_SGU] = np.stack([SG, SU], axis=1).reshape(N_SGU, 128, 128)
    DT = down_w.transpose(0, 2, 1).reshape(NE, FCH, 128, KC, 128).transpose(3, 0, 1, 2, 4)
    SD = sh_d.T.reshape(FSCH, 128, KC, 128).transpose(2, 0, 1, 3)
    dn = np.concatenate([DT.reshape(KC, NE * FCH, 128, 128), SD], axis=1)
    tiles[N_GU + N_SGU:SEG_QK0] = dn.reshape(N_DOWN, 128, 128)
    WQ = w_qkv.reshape(KC, 128, 48, 128).transpose(2, 0, 1, 3)
    tiles[SEG_QK0:SEG_QK0 + 512] = WQ[np.r_[16:32, 0:16]].reshape(512, 128, 128)
    tiles[SEG_V0:SEG_V0 + 256] = WQ[32:48].transpose(1, 0, 2, 3).reshape(256, 128, 128)
    tiles[SEG_WO0:] = w_o.reshape(KC, 128, KC, 128).transpose(2, 0, 1, 3).reshape(256, 128, 128)
    return np.ascontiguousarray(tiles.transpose(1, 0, 2).reshape(128, NTILES * 128))


def _host_inputs(inputs):
    x = np.asarray(inputs["L0_attn_x"])
    wblob = _build_wblob(
        np.asarray(inputs["L0_attn_w_qkv"]), np.asarray(inputs["L0_attn_w_o"]),
        np.asarray(inputs["L0_ffn_gate_w"]), np.asarray(inputs["L0_ffn_up_w"]),
        np.asarray(inputs["L0_ffn_down_w"]), np.asarray(inputs["L0_ffn_sh_gate_w"]),
        np.asarray(inputs["L0_ffn_sh_up_w"]), np.asarray(inputs["L0_ffn_sh_down_w"]))
    gwpm = np.ascontiguousarray(
        np.asarray(inputs["L0_ffn_gate_weight"]).T.reshape(KC, 128, NE).transpose(1, 0, 2).reshape(128, KC * NE))
    consts = np.zeros((128, 1664), dtype=np.float32)
    consts[:, 1536:1664] = NEG
    for n in range(NE):
        consts[n, 128 * n:128 * (n + 1)] = 1.0
    consts[np.arange(8), 1024 + np.arange(8)] = 1.0
    consts[np.arange(128), 1152 + np.arange(128)] = 1.0
    consts[:, 1424:1441] = 1.0
    consts[0, 1281:1409] = 1.0
    tri = np.where(np.arange(128)[:, None] <= np.arange(128)[None, :], 0.0, NEG).astype(np.float32)
    inv_freq = 1.0 / 10000.0 ** (np.arange(0, HD, 2, dtype=np.float64) / HD)
    per_core = []
    for c in range(8):
        b, p = c // 2, c % 2
        i = np.arange(TLOC)
        pos = 256 * (i // 128) + 128 * p + (i % 128)
        xc = np.ascontiguousarray(x[b][pos, :].T)
        fr = pos[None, :] * inv_freq[:, None]
        per_core.append({
            "x_in": xc, "wblob": wblob, "gwpm": gwpm, "consts": consts,
            "mask0": np.ascontiguousarray(tri if p == 0 else np.zeros((128, 128), np.float32)),
            "mask1": np.ascontiguousarray(np.full((128, 128), NEG, np.float32) if p == 0 else tri),
            "ctab": np.ascontiguousarray(np.tile(np.cos(fr).astype(np.float32), (2, 1))),
            "stab": np.ascontiguousarray(np.tile(np.sin(fr).astype(np.float32), (2, 1))),
        })
    return per_core


def _assemble(results):
    y = np.empty((4, 4096, D), dtype=np.float32)
    for c in range(8):
        b, p = c // 2, c % 2
        i = np.arange(TLOC)
        pos = 256 * (i // 128) + 128 * p + (i % 128)
        y[b][pos, :] = results[c]["out_x"].T
    return y


def kernel(**inputs):
    nc = build_program(_PHASES)
    in_maps = _host_inputs(inputs)
    res = run_bass_kernel_spmd(nc, in_maps, core_ids=list(range(8)))
    return _assemble(res.results)
